# version 21
# baseline (speedup 1.0000x reference)
import os

import numpy as np
from contextlib import ExitStack

if os.environ.get("BASS_SIM_FP64") == "1":
    from concourse import bass_interp as _bi

    _orig_coresim_init = _bi.CoreSim.__init__

    def _coresim_init_allow_overflow(self, *a, **kw):
        kw.setdefault("require_finite", False)
        kw.setdefault("require_nnan", False)
        return _orig_coresim_init(self, *a, **kw)

    _bi.CoreSim.__init__ = _coresim_init_allow_overflow

import concourse.bass as bass
import concourse.mybir as mybir
from concourse.bass_utils import run_bass_kernel_spmd

F32 = mybir.dt.float32
I32 = mybir.dt.int32
BF16 = mybir.dt.bfloat16
AF = mybir.ActivationFunctionType
ALU = mybir.AluOpType
AX = mybir.AxisListType.X

D, FF, TPC, TB = 1024, 4096, 2048, 1024
NC, NF = 8, 32
S64 = float(2.0 ** 64)
S64I = float(2.0 ** -64)
INV1024 = 1.0 / 1024.0
INV127 = 1.0 / 127.0
MAGIC = 12582912.0


class Trk:
    def __init__(self, ctx, nc, name):
        self.h = ctx.enter_context(nc.semaphore(name))
        self.v = 0

    def inc(self, inst, k=1):
        inst.then_inc(self.h, k)
        self.v += k
        return self.v


def build_program(stage="full"):
    nc = bass.Bass("TRN2", target_bir_lowering=False, detect_race_conditions=False)

    x_in = nc.dram_tensor("x_in", [TPC, D], F32, kind="ExternalInput").ap()
    w1_in = nc.dram_tensor("w1_in", [FF, D], F32, kind="ExternalInput").ap()
    b1_in = nc.dram_tensor("b1_in", [FF], F32, kind="ExternalInput").ap()
    w2_in = nc.dram_tensor("w2_in", [D, FF], F32, kind="ExternalInput").ap()
    b2_in = nc.dram_tensor("b2_in", [D], F32, kind="ExternalInput").ap()
    wx_in = nc.dram_tensor("wx_in", [2 * D, D], F32, kind="ExternalInput").ap()
    bx_in = nc.dram_tensor("bx_in", [2 * D], F32, kind="ExternalInput").ap()
    wdt_in = nc.dram_tensor("wdt_in", [D, D], F32, kind="ExternalInput").ap()
    bdt_in = nc.dram_tensor("bdt_in", [D], F32, kind="ExternalInput").ap()
    alog_in = nc.dram_tensor("alog_in", [D], F32, kind="ExternalInput").ap()
    gsl_in = nc.dram_tensor("gsl_in", [11 * 128, D], F32, kind="ExternalInput").ap()
    ident_in = nc.dram_tensor("ident_in", [128, 128], F32, kind="ExternalInput").ap()
    mu_in = nc.dram_tensor("mu_in", [128, 1], I32, kind="ExternalInput").ap()
    sbit_in = nc.dram_tensor("sbit_in", [128, 48], I32, kind="ExternalInput").ap()
    y_out = nc.dram_tensor("y_out", [TPC, D], F32, kind="ExternalOutput").ap()

    wq1_s = nc.dram_tensor("wq1_s", [FF, D], BF16, kind="Internal").ap()
    wq2_s = nc.dram_tensor("wq2_s", [D, FF], BF16, kind="Internal").ap()
    wq1T_s = nc.dram_tensor("wq1T_s", [D, FF], BF16, kind="Internal").ap()
    wq2T_s = nc.dram_tensor("wq2T_s", [FF, D], BF16, kind="Internal").ap()
    wx_s = nc.dram_tensor("wx_s", [2 * D, D], BF16, kind="Internal").ap()
    wdt_s = nc.dram_tensor("wdt_s", [D, D], BF16, kind="Internal").ap()
    u_sp = nc.dram_tensor("u_sp", [16 * 128, TB], F32, kind="Internal").ap()
    p_sp = nc.dram_tensor("p_sp", [16 * 128, TB], F32, kind="Internal").ap()
    c_sp = nc.dram_tensor("c_sp", [16 * 128, TB], F32, kind="Internal").ap()
    bounce = nc.dram_tensor("bounce", [8, 128], F32, kind="Internal").ap()
    cc_in = nc.dram_tensor("cc_in", [128, 16], F32, kind="Internal")
    cc_out = nc.dram_tensor("cc_out", [8 * 128, 16], F32, kind="Internal")
    cc2_in = nc.dram_tensor("cc2_in", [128, 4], F32, kind="Internal")
    cc2_out = nc.dram_tensor("cc2_out", [8 * 128, 4], F32, kind="Internal")

    ctx = ExitStack()
    SBUF = lambda name, shape, dt, **kw: ctx.enter_context(nc.sbuf_tensor(name, shape, dt, **kw))

    resid = {}
    for c in range(NC):
        for tb in range(2):
            resid[(c, tb)] = SBUF(f"res_{c}_{tb}", [128, TB], F32)
    spare = [SBUF("spare", [128, TB], F32)]
    ident = SBUF("ident", [128, 128], F32)
    ones_r = SBUF("ones_r", [1, 128], F32)
    ones_cb = SBUF("ones_cb", [128, 1], BF16)
    ones_cf = SBUF("ones_cf", [128, 1], F32)
    b1sb = SBUF("b1sb", [128, NF], F32)
    hb2sb = SBUF("hb2sb", [128, NC], F32)
    hb2row = SBUF("hb2row", [1, D], F32)
    bxsb = SBUF("bxsb", [128, 16], F32)
    bdtsb = SBUF("bdtsb", [128, NC], F32)
    asb = SBUF("asb", [128, NC], F32)
    musb = SBUF("musb", [128, 1], I32)
    sbits = SBUF("sbits", [128, 48], I32)
    zcol = SBUF("zcol", [128, 1], F32)
    bigcol = SBUF("bigcol", [128, 1], F32)
    zeros_h = SBUF("zeros_h", [128, 512], F32)
    ends = SBUF("ends", [128, 16], F32)
    chainP = SBUF("chainP", [128, NC], F32)
    chainS = SBUF("chainS", [128, NC], F32)
    chainS2 = SBUF("chainS2", [128, NC], F32)
    r1s_t = SBUF("r1s_t", [128, NC], F32)
    q2_t = SBUF("q2_t", [128, NC], F32)
    wsrow = SBUF("wsrow", [1, 16], F32)
    rbc = SBUF("rbc", [128, 4], F32)
    gacp = SBUF("gacp", [128, 16], F32)
    gath2 = SBUF("gath2", [128, 32], F32)

    pA = ctx.enter_context(nc.psum_tensor("pA", [128, 1024], F32))
    pB = ctx.enter_context(nc.psum_tensor("pB", [128, 1024], F32))
    pC = ctx.enter_context(nc.psum_tensor("pC", [128, 1024], F32))
    pD = ctx.enter_context(nc.psum_tensor("pD", [128, 1024], F32))
    PS4 = [pA, pB, pC, pD]

    def _fence(eng, F):
        eng.wait_ge(pe.h, F[0])
        eng.wait_ge(dv.h, F[1])
        eng.wait_ge(ac.h, F[2])

    sy1 = Trk(ctx, nc, "sy1")
    sy2 = Trk(ctx, nc, "sy2")
    sy3 = Trk(ctx, nc, "sy3")
    sy4 = Trk(ctx, nc, "sy4")
    sy5 = Trk(ctx, nc, "sy5")
    sc1 = Trk(ctx, nc, "sc1")
    sc2 = Trk(ctx, nc, "sc2")
    gp1 = Trk(ctx, nc, "gp1")
    gp2 = Trk(ctx, nc, "gp2")
    gp3 = Trk(ctx, nc, "gp3")
    gp4 = Trk(ctx, nc, "gp4")
    sy6 = Trk(ctx, nc, "sy6")
    pe = Trk(ctx, nc, "pe")
    dv = Trk(ctx, nc, "dv")
    ac = Trk(ctx, nc, "ac")
    st = Trk(ctx, nc, "st")
    ccs = Trk(ctx, nc, "ccs")

    with ExitStack() as b0:
        xtm = b0.enter_context(nc.sbuf_tensor("xtm", [128, 4 * D], F32))
        gb1 = b0.enter_context(nc.sbuf_tensor("gb1", [128, 2 * D], F32))
        gca = b0.enter_context(nc.sbuf_tensor("gca", [128, 32], F32))
        atmp = b0.enter_context(nc.sbuf_tensor("atmp", [128, 2 * NC], F32))
        blk = b0.enter_context(nc.Block())
        F_ = (pe.v, dv.v, ac.v)

        pe0, dv0, ac0 = pe.v, dv.v, ac.v
        gp1_0, gp2_0 = gp1.v, gp2.v
        xs_sem = {0: sy1, 2: sy2, 1: sc1, 3: sc2}
        ga_sem = [sy3, sy4]
        sy_x = [None] * 16
        sy_g1 = [None] * 11
        for i in range(16):
            sy_x[i] = (xs_sem[i % 4], xs_sem[i % 4].v + (i // 4 + 1) * 16)
        for i in range(11):
            sy_g1[i] = (ga_sem[i % 2], ga_sem[i % 2].v + (i // 2 + 1) * 16)
        sy5_setup_target = sy5.v + 76 * 16
        gac = gacp
        ms_gam = [0]

        @blk.sync
        def _(sync):
            _fence(sync, F_)
            sy5.inc(sync.dma_start(out=ident[:], in_=ident_in), 16)
            sy5.inc(sync.dma_start(out=musb[:], in_=mu_in), 16)
            sy5.inc(sync.dma_start(out=sbits[:], in_=sbit_in), 16)
            for c in range(NF):
                sy5.inc(sync.dma_start(out=b1sb[:, c:c + 1], in_=b1_in[c * 128:(c + 1) * 128, None]), 16)
            for c in range(NC):
                sy5.inc(sync.dma_start(out=hb2sb[:, c:c + 1], in_=b2_in[c * 128:(c + 1) * 128, None]), 16)
            sy5.inc(sync.dma_start(out=hb2row[:], in_=b2_in[None, :]), 16)
            for c in range(16):
                sy5.inc(sync.dma_start(out=bxsb[:, c:c + 1], in_=bx_in[c * 128:(c + 1) * 128, None]), 16)
            for c in range(NC):
                sy5.inc(sync.dma_start(out=bdtsb[:, c:c + 1], in_=bdt_in[c * 128:(c + 1) * 128, None]), 16)
            for c in range(NC):
                sy5.inc(sync.dma_start(out=atmp[:, c:c + 1], in_=alog_in[c * 128:(c + 1) * 128, None]), 16)
            assert sy5.v == sy5_setup_target
            for tt in range(0, 16, 2):
                if tt >= 4:
                    sync.wait_ge(pe.h, pe0 + tt - 3)
                v = xs_sem[tt % 4].inc(sync.dma_start(
                    out=xtm[:, (tt % 4) * D:(tt % 4 + 1) * D],
                    in_=x_in[tt * 128:(tt + 1) * 128, :]), 16)
                assert (xs_sem[tt % 4], v) == sy_x[tt]
            for i in range(11):
                if i >= 2:
                    sync.wait_ge(dv.h, dv0 + 16 + (i - 1))
                v = ga_sem[i % 2].inc(sync.dma_start(
                    out=gb1[:, (i % 2) * D:(i % 2 + 1) * D],
                    in_=gsl_in[i * 128:(i + 1) * 128, :]), 16)
                assert (ga_sem[i % 2], v) == sy_g1[i]
            for sem in (sy1, sy2, sy3, sy4, sy5):
                sync.wait_ge(sem.h, sem.v)

        @blk.tensor
        def _(tensor):
            _fence(tensor, F_)
            tensor.wait_ge(sy5.h, sy5_setup_target)
            for tt in range(16):
                tensor.wait_ge(sy_x[tt][0].h, sy_x[tt][1])
                if tt >= 4:
                    tensor.wait_ge(dv.h, dv0 + tt - 3)
                ps = PS4[tt % 4]
                inst = None
                for c in range(NC):
                    inst = tensor.matmul(ps[:, c * 128:(c + 1) * 128],
                                         xtm[:, (tt % 4) * D + c * 128:(tt % 4) * D + (c + 1) * 128],
                                         ident[:], start=(c % 4 == 0), stop=(c % 4 == 3))
                pe.inc(inst, 1)

        @blk.vector
        def _(vector):
            _fence(vector, F_)
            vector.memset(ones_r[:], 1.0)
            vector.memset(ones_cb[:], 1.0)
            vector.memset(ones_cf[:], 1.0)
            vector.memset(zcol[:], 0.0)
            vector.memset(bigcol[:], S64)
            vector.memset(zeros_h[:], 0.0)
            vector.memset(gac[:], 0.0)
            vector.memset(chainP[:, 0:NC], S64)
            vector.memset(chainS[:], 0.0)
            for tt in range(16):
                vector.wait_ge(pe.h, pe0 + tt + 1)
                ps = PS4[tt % 4]
                tb, tl = tt // 8, tt % 8
                inst = None
                for c in range(NC):
                    inst = vector.tensor_copy(resid[(c, tb)][:, tl * 128:(tl + 1) * 128],
                                              ps[:, c * 128:(c + 1) * 128])
                dv.inc(inst, 1)
            assert dv.v == dv0 + 16
            for i in range(11):
                vector.wait_ge(sy_g1[i][0].h, sy_g1[i][1])
                dv.inc(vector.tensor_reduce(gca[:, i:i + 1], gb1[:, (i % 2) * D:(i % 2 + 1) * D],
                                            AX, ALU.add, apply_absolute_value=True), 1)
            vector.wait_ge(dv.h, dv.v)
            vector.tensor_reduce(gac[:, 0:1], gca[:, 0:4], AX, ALU.add)
            vector.tensor_reduce(gac[:, 1:2], gca[:, 4:8], AX, ALU.add)
            vector.tensor_reduce(gac[:, 2:3], gca[:, 8:10], AX, ALU.add)
            ms_gam[0] = dv.inc(vector.tensor_copy(gac[:, 3:4], gca[:, 10:11]), 1)
            vector.wait_ge(ac.h, ac0 + 3)
            v = dv.inc(vector.tensor_scalar(asb[:], atmp[:, 0:NC], 0.0, None, ALU.max), 1)
            vector.wait_ge(dv.h, v)
            v = dv.inc(vector.scalar_tensor_tensor(atmp[:, 0:NC], asb[:], -1.0, atmp[:, NC:2 * NC],
                                                   ALU.mult, ALU.subtract), 1)
            vector.wait_ge(dv.h, v)
            dv.inc(vector.tensor_copy(asb[:], atmp[:, 0:NC]), 1)

        @blk.scalar
        def _(scalar):
            _fence(scalar, F_)
            for tt in range(1, 16, 2):
                if tt >= 4:
                    scalar.wait_ge(pe.h, pe0 + tt - 3)
                v = xs_sem[tt % 4].inc(scalar.dma_start(
                    out=xtm[:, (tt % 4) * D:(tt % 4 + 1) * D],
                    in_=x_in[tt * 128:(tt + 1) * 128, :]), 16)
                assert (xs_sem[tt % 4], v) == sy_x[tt]
            for sem in (sc1, sc2):
                scalar.wait_ge(sem.h, sem.v)
            scalar.wait_ge(sy5.h, sy5_setup_target)
            v = ac.inc(scalar.activation(atmp[:, NC:2 * NC], atmp[:, 0:NC], AF.Abs), 1)
            scalar.wait_ge(ac.h, v)
            v = ac.inc(scalar.activation(asb[:], atmp[:, NC:2 * NC], AF.Exp, scale=-1.0), 1)
            scalar.wait_ge(ac.h, v)
            ac.inc(scalar.activation(atmp[:, NC:2 * NC], asb[:], AF.Ln, bias=1.0), 1)

        @blk.gpsimd
        def _(gpsimd):
            _fence(gpsimd, F_)
            gpsimd.wait_ge(dv.h, ms_gam[0])
            v = gp1.inc(gpsimd.dma_start(out=cc2_in.ap(), in_=gac[:, 0:4]), 16)
            gpsimd.wait_ge(gp1.h, v)
            v = ccs.inc(gpsimd.collective_compute(
                "AllGather", ALU.bypass, replica_groups=[list(range(8))],
                ins=[cc2_in.ap().opt()], outs=[cc2_out.ap().opt()]), 1)
            gpsimd.wait_ge(ccs.h, v)
            for r in range(8):
                gp2.inc(gpsimd.dma_start(out=gath2[:, r * 4:(r + 1) * 4],
                                         in_=cc2_out.ap()[r * 128:(r + 1) * 128, :]), 16)
            gpsimd.wait_ge(gp2.h, gp2.v)



    with nc.Block() as blk:
        F_ = (pe.v, dv.v, ac.v)
        pe0, dv0 = pe.v, dv.v
        nelem = {0: FF * D, 1: D * FF, 2: 2 * D * D, 3: D * D}
        ms = {}

        @blk.vector
        def _(vector):
            _fence(vector, F_)
            vector.wait_ge(pe.h, pe0 + 1)
            v = dv.inc(vector.tensor_copy(wsrow[0:1, 0:4], pA[0:1, 0:4]), 1)
            vector.wait_ge(dv.h, v)
            for wi in range(4):
                v = dv.inc(vector.tensor_scalar(wsrow[0:1, 4 + wi:5 + wi], wsrow[0:1, wi:wi + 1],
                                                1.0 / nelem[wi], 1e-5, ALU.mult, ALU.add), 1)
            vector.wait_ge(dv.h, v)
            ms["recip"] = dv.inc(vector.reciprocal(wsrow[0:1, 8:12], wsrow[0:1, 4:8]), 1)
            vector.wait_ge(pe.h, pe0 + 2)
            dv.inc(vector.tensor_copy(rbc[:], pB[:, 0:4]), 1)
            vector.tensor_scalar(hb2sb[:], hb2sb[:], 0.5, None, ALU.mult)
            dv.inc(vector.tensor_scalar(hb2row[:], hb2row[:], 0.5, None, ALU.mult), 1)

        @blk.tensor
        def _(tensor):
            _fence(tensor, F_)
            inst = None
            for r in range(8):
                inst = tensor.matmul(pA[0:1, 0:4], ones_cf[:], gath2[:, r * 4:(r + 1) * 4],
                                     start=(r == 0), stop=(r == 7))
            pe.inc(inst, 1)
            tensor.wait_ge(dv.h, ms["recip"])
            pe.inc(tensor.matmul(pB[:, 0:4], ones_r[:], wsrow[0:1, 8:12], start=True, stop=True), 1)

    with ExitStack() as b0c:
        srcA = b0c.enter_context(nc.sbuf_tensor("srcA", [128, 4 * D], F32))
        qiA = b0c.enter_context(nc.sbuf_tensor("qiA", [128, D], F32))
        qbA = b0c.enter_context(nc.sbuf_tensor("qbA", [128, 4 * D], BF16))
        srcB = b0c.enter_context(nc.sbuf_tensor("srcB", [128, 4 * D], F32))
        qiB = b0c.enter_context(nc.sbuf_tensor("qiB", [128, D], F32))
        qfB = b0c.enter_context(nc.sbuf_tensor("qfB", [128, D], F32))
        qbB = b0c.enter_context(nc.sbuf_tensor("qbB", [128, 4 * D], BF16))
        rb2 = b0c.enter_context(nc.sbuf_tensor("rb2", [128, 4 * 1024], BF16))
        blk = b0c.enter_context(nc.Block())
        F_ = (pe.v, dv.v, ac.v)

        dv0, sc1_0 = dv.v, sc1.v
        la_sem = [sy1, sy2]
        sy_loadA = [(la_sem[i % 2], la_sem[i % 2].v + (i // 2 + 1) * 16) for i in range(32)]
        sc_loadB2 = [(sy1, sy1.v + 16 * 16 + (i // 2 + 1) * 16) if i % 2 == 0
                     else (sc2, sc2.v + (i // 2 + 1) * 16) for i in range(64)]
        st_a0 = sy3.v
        tilesB = []
        for r0 in range(0, D, 128):
            for c0 in range(0, FF, 1024):
                tilesB.append((1, w2_in[r0:r0 + 128, c0:c0 + 1024], wq2_s[r0:r0 + 128, c0:c0 + 1024], True))
        for r0 in range(0, 2 * D, 128):
            tilesB.append((2, wx_in[r0:r0 + 128, :], wx_s[r0:r0 + 128, :], False))
        for r0 in range(0, D, 128):
            tilesB.append((3, wdt_in[r0:r0 + 128, :], wdt_s[r0:r0 + 128, :], False))
        nB = len(tilesB)
        order = [("A", i) for i in range(32)] + [("B", j) for j in range(nB)]
        msA, msB = [0] * 32, [0] * nB
        for k, (s, i) in enumerate(order):
            (msA if s == "A" else msB)[i] = dv0 + k + 1
        sy4_0, gp2_0b = sy4.v, gp2.v

        @blk.sync
        def _(sync):
            _fence(sync, F_)
            for i in range(32):
                if i >= 4:
                    sync.wait_ge(dv.h, msA[i - 4])
                v = la_sem[i % 2].inc(sync.dma_start(
                    out=srcA[:, (i % 4) * D:(i % 4 + 1) * D],
                    in_=w1_in[i * 128:(i + 1) * 128, :]), 16)
                assert (la_sem[i % 2], v) == sy_loadA[i]
                if i >= 1:
                    sync.wait_ge(dv.h, msA[i - 1])
                    sy3.inc(sync.dma_start(out=wq1_s[(i - 1) * 128:i * 128, :],
                                           in_=qbA[:, ((i - 1) % 4) * D:((i - 1) % 4 + 1) * D]), 16)
            sync.wait_ge(dv.h, msA[31])
            sy3.inc(sync.dma_start(out=wq1_s[31 * 128:32 * 128, :],
                                   in_=qbA[:, (31 % 4) * D:(31 % 4 + 1) * D]), 16)
            for i in range(0, len(tilesB), 2):
                if i >= 4:
                    sync.wait_ge(dv.h, msB[i - 4])
                v = sy1.inc(sync.dma_start(
                    out=srcB[:, (i % 4) * D:(i % 4 + 1) * D], in_=tilesB[i][1]), 16)
                assert (sy1, v) == sc_loadB2[i]
            for sem in (sy1, sy2, sy3):
                sync.wait_ge(sem.h, sem.v)

        @blk.vector
        def _(vector):
            _fence(vector, F_)
            for s, i in order:
                if s == "A":
                    vector.wait_ge(sy_loadA[i][0].h, sy_loadA[i][1])
                    if i >= 4:
                        vector.wait_ge(sy3.h, st_a0 + (i - 3) * 16)
                    src = srcA[:, (i % 4) * D:(i % 4 + 1) * D]
                    vector.tensor_scalar(qiA[:], src, rbc[:, 0:1], None, ALU.mult)
                    vector.tensor_scalar(qiA[:], qiA[:], MAGIC, MAGIC, ALU.add, ALU.subtract)
                    v = dv.inc(vector.tensor_scalar(qbA[:, (i % 4) * D:(i % 4 + 1) * D], qiA[:],
                                                    1.0, -1.0, ALU.min, ALU.max), 1)
                    assert v == msA[i]
                else:
                    wi, _, dst, halve = tilesB[i]
                    vector.wait_ge(sc_loadB2[i][0].h, sc_loadB2[i][1])
                    if i >= 4:
                        vector.wait_ge(sc1.h, sc1_0 + (i - 3) * 16)
                    src = srcB[:, (i % 4) * D:(i % 4 + 1) * D]
                    vector.tensor_scalar(qiB[:], src, rbc[:, wi:wi + 1], None, ALU.mult)
                    vector.tensor_scalar(qiB[:], qiB[:], MAGIC, MAGIC, ALU.add, ALU.subtract)
                    if halve:
                        vector.tensor_scalar(qfB[:], qiB[:], 1.0, -1.0, ALU.min, ALU.max)
                        v = dv.inc(vector.tensor_scalar(qbB[:, (i % 4) * D:(i % 4 + 1) * D], qfB[:],
                                                        0.5, None, ALU.mult), 1)
                    else:
                        v = dv.inc(vector.tensor_scalar(qbB[:, (i % 4) * D:(i % 4 + 1) * D], qiB[:],
                                                        1.0, -1.0, ALU.min, ALU.max), 1)
                    assert v == msB[i]

        @blk.scalar
        def _(scalar):
            _fence(scalar, F_)
            def _loadB_odd(j):
                if j >= 4:
                    scalar.wait_ge(dv.h, msB[j - 4])
                v = sc2.inc(scalar.dma_start(
                    out=srcB[:, (j % 4) * D:(j % 4 + 1) * D], in_=tilesB[j][1]), 16)
                assert (sc2, v) == sc_loadB2[j]
            _loadB_odd(1)
            _loadB_odd(3)
            for i, (wi, _, dst, halve) in enumerate(tilesB):
                if i % 2 == 1 and i + 4 < len(tilesB):
                    _loadB_odd(i + 4)
                scalar.wait_ge(dv.h, msB[i])
                sc1.inc(scalar.dma_start(out=dst, in_=qbB[:, (i % 4) * D:(i % 4 + 1) * D]), 16)
                if 28 <= i <= 31:
                    scalar.wait_ge(sc1.h, sc1_0 + (i + 1) * 16)
                    for f in range((i - 28) * 8, (i - 28) * 8 + 8):
                        if f >= 4:
                            scalar.wait_ge(gp2.h, gp2_0b + (f - 3) * 16)
                        sy4.inc(scalar.dma_start(
                            out=rb2[:, (f % 4) * 1024:(f % 4 + 1) * 1024],
                            in_=wq2_s[0:1024, f * 128:(f + 1) * 128], transpose=True), 16)
            scalar.wait_ge(sc1.h, sc1.v)
            scalar.wait_ge(sc2.h, sc2.v)
            scalar.wait_ge(sy4.h, sy4.v)

        @blk.gpsimd
        def _(gpsimd):
            _fence(gpsimd, F_)
            for f in range(32):
                gpsimd.wait_ge(sy4.h, sy4_0 + (f + 1) * 16)
                gp2.inc(gpsimd.dma_start(out=wq2T_s[f * 128:(f + 1) * 128, :],
                                         in_=rb2[:, (f % 4) * 1024:(f % 4 + 1) * 1024]), 16)
            gpsimd.wait_ge(gp2.h, gp2.v)


    def emit_ln(tb, xln, tag):
        with ExitStack() as sscope:
            sq = sscope.enter_context(nc.sbuf_tensor(f"lnsq{tag}", [128, 2 * TB], BF16))
            rowb = sscope.enter_context(nc.sbuf_tensor(f"lnrow{tag}", [1, 4 * TB], F32))
            tmpt = sscope.enter_context(nc.sbuf_tensor(f"lntmp{tag}", [128, TB], F32))
            blk = sscope.enter_context(nc.Block())
            F_ = (pe.v, dv.v, ac.v)
            pe0, dv0, ac0 = pe.v, dv.v, ac.v
            R = lambda i, o=0, w=TB: rowb[0:1, i * TB + o:i * TB + o + w]
            ms = {}

            @blk.vector
            def _(vector):
                _fence(vector, F_)
                vector.wait_ge(pe.h, pe0 + 8)
                v = dv.inc(vector.tensor_copy(R(0), pC[0:1, 0:TB]), 1)
                v = dv.inc(vector.tensor_copy(R(1), pD[0:1, 0:TB]), 1)
                vector.wait_ge(dv.h, v)
                v = dv.inc(vector.tensor_scalar(R(2), R(0), INV1024, None, ALU.mult), 1)
                v = dv.inc(vector.tensor_scalar(R(3), R(1), INV1024, None, ALU.mult), 1)
                vector.wait_ge(dv.h, v)
                v = dv.inc(vector.tensor_tensor(R(1), R(2), R(2), ALU.mult), 1)
                vector.wait_ge(dv.h, v)
                v = dv.inc(vector.tensor_tensor(R(0), R(3), R(1), ALU.subtract), 1)
                vector.wait_ge(dv.h, v)
                ms["vareps"] = dv.inc(vector.tensor_scalar(R(1), R(0), 1e-5, None, ALU.add), 1)
                vector.wait_ge(ac.h, ac0 + 10)
                ms["brow"] = dv.inc(vector.scalar_tensor_tensor(R(3), R(2), -1.0, R(0),
                                                               ALU.mult, ALU.mult), 1)
                vector.wait_ge(pe.h, pe0 + 9)
                for c in range(NC):
                    v = dv.inc(vector.tensor_tensor(tmpt[:], resid[(c, tb)][:], pC[:, 0:TB], ALU.mult), 1)
                    vector.wait_ge(dv.h, v)
                    dv.inc(vector.tensor_tensor(xln[c][:], tmpt[:], pD[:, 0:TB], ALU.add), 1)

            @blk.scalar
            def _(scalar):
                _fence(scalar, F_)
                for c in range(NC):
                    if c >= 2:
                        scalar.wait_ge(pe.h, pe0 + c - 1)
                    ac.inc(scalar.activation(sq[:, (c % 2) * TB:(c % 2 + 1) * TB],
                                             resid[(c, tb)][:], AF.Square), 1)
                scalar.wait_ge(dv.h, ms["vareps"])
                v = ac.inc(scalar.activation(R(3), R(1), AF.Ln), 1)
                scalar.wait_ge(ac.h, v)
                v = ac.inc(scalar.activation(R(0), R(3), AF.Exp, scale=-0.5), 1)
                assert v == ac0 + 10

            @blk.tensor
            def _(tensor):
                _fence(tensor, F_)
                for c in range(NC):
                    tensor.wait_ge(ac.h, ac0 + c + 1)
                    rs_ = resid[(c, tb)]
                    ss = sq[:, (c % 2) * TB:(c % 2 + 1) * TB]
                    tensor.matmul(pC[0:1, 0:512], ones_cf[:], rs_[:, 0:512], start=(c == 0), stop=(c == 7))
                    tensor.matmul(pC[0:1, 512:1024], ones_cf[:], rs_[:, 512:1024], start=(c == 0), stop=(c == 7))
                    tensor.matmul(pD[0:1, 0:512], ones_cb[:], ss[:, 0:512], start=(c == 0), stop=(c == 7))
                    inst = tensor.matmul(pD[0:1, 512:1024], ones_cb[:], ss[:, 512:1024], start=(c == 0), stop=(c == 7))
                    pe.inc(inst, 1)
                tensor.wait_ge(ac.h, ac0 + 10)
                tensor.wait_ge(dv.h, ms["brow"])
                tensor.matmul(pC[:, 0:512], ones_r[:], R(0, 0, 512), start=True, stop=True)
                tensor.matmul(pC[:, 512:1024], ones_r[:], R(0, 512, 512), start=True, stop=True)
                tensor.matmul(pD[:, 0:512], ones_r[:], R(3, 0, 512), start=True, stop=True)
                inst = tensor.matmul(pD[:, 512:1024], ones_r[:], R(3, 512, 512), start=True, stop=True)
                pe.inc(inst, 1)

    def emit_mm1(tb, xln, H, tag):
        with ExitStack() as s:
            wslot = s.enter_context(nc.sbuf_tensor(f"w1s{tag}", [128, 2 * 4096], BF16))
            blk = s.enter_context(nc.Block())
            F_ = (pe.v, dv.v, ac.v)
            pe0, ac0 = pe.v, ac.v
            fg_sem = [sy1, sy2]
            sy_fg = [(fg_sem[fg % 2], fg_sem[fg % 2].v + (fg // 2 + 1) * 128) for fg in range(8)]

            @blk.sync
            def _(sync):
                _fence(sync, F_)
                for fg in range(8):
                    if fg >= 2:
                        sync.wait_ge(pe.h, pe0 + (fg - 1) * 4)
                    v = None
                    for k in range(8):
                        v = fg_sem[fg % 2].inc(sync.dma_start(
                            out=wslot[:, (fg % 2) * 4096 + k * 512:(fg % 2) * 4096 + (k + 1) * 512],
                            in_=wq1_s[fg * 512:(fg + 1) * 512, k * 128:(k + 1) * 128],
                            transpose=True), 16)
                    assert (fg_sem[fg % 2], v) == sy_fg[fg]
                for sem in (sy1, sy2):
                    sync.wait_ge(sem.h, sem.v)

            @blk.tensor
            def _(tensor):
                _fence(tensor, F_)
                for fg in range(8):
                    tensor.wait_ge(sy_fg[fg][0].h, sy_fg[fg][1])
                    for fcl in range(4):
                        fc = fg * 4 + fcl
                        ps = pA if fc % 2 == 0 else pB
                        if fc >= 2:
                            tensor.wait_ge(ac.h, ac0 + fc - 1)
                        inst = None
                        for k in range(8):
                            base = (fg % 2) * 4096 + k * 512 + fcl * 128
                            tensor.matmul(ps[:, 0:512], wslot[:, base:base + 128],
                                          xln[k][:, 0:512], start=(k == 0), stop=(k == 7))
                            inst = tensor.matmul(ps[:, 512:1024], wslot[:, base:base + 128],
                                                 xln[k][:, 512:1024], start=(k == 0), stop=(k == 7))
                        pe.inc(inst, 1)

            @blk.scalar
            def _(scalar):
                _fence(scalar, F_)
                for fc in range(32):
                    scalar.wait_ge(pe.h, pe0 + fc + 1)
                    ps = pA if fc % 2 == 0 else pB
                    ac.inc(scalar.activation(H[fc][:], ps[:], AF.Relu,
                                             bias=b1sb[:, fc:fc + 1]), 1)

    def emit_mm2a(tb, H, tag):
        with ExitStack() as s:
            wslot = s.enter_context(nc.sbuf_tensor(f"w2s{tag}", [128, 4 * 1024], BF16))
            blk = s.enter_context(nc.Block())
            F_ = (pe.v, dv.v, ac.v)
            pe0, dv0 = pe.v, dv.v
            k_sem = {0: sy1, 2: sy2, 1: sc1, 3: sc2}
            sy_k = {}
            cnt = {0: sy1.v, 2: sy2.v, 1: sc1.v, 3: sc2.v}
            for sub in range(2):
                for k in range(32):
                    i = sub * 32 + k
                    cnt[i % 4] += 16
                    sy_k[(sub, k)] = (k_sem[i % 4], cnt[i % 4])

            def _loads(eng, par):
                for sub in range(2):
                    for k in range(32):
                        i = sub * 32 + k
                        if par is not None and i % 2 != par:
                            continue
                        if i >= 4:
                            eng.wait_ge(pe.h, pe0 + i - 3)
                        v = k_sem[i % 4].inc(eng.dma_start(
                            out=wslot[:, (i % 4) * 1024:(i % 4 + 1) * 1024],
                            in_=wq2T_s[k * 128:(k + 1) * 128, 0:1024]), 16)
                        assert (k_sem[i % 4], v) == sy_k[(sub, k)]

            @blk.sync
            def _(sync):
                _fence(sync, F_)
                _loads(sync, 0)
                for sem in (sy1, sy2):
                    sync.wait_ge(sem.h, sem.v)

            @blk.scalar
            def _(scalar):
                _fence(scalar, F_)
                _loads(scalar, 1)
                for sem in (sc1, sc2):
                    scalar.wait_ge(sem.h, sem.v)

            @blk.tensor
            def _(tensor):
                _fence(tensor, F_)
                for sub in range(2):
                    if sub == 1:
                        tensor.wait_ge(dv.h, dv0 + 1)
                    for k in range(32):
                        i = sub * 32 + k
                        tensor.wait_ge(sy_k[(sub, k)][0].h, sy_k[(sub, k)][1])
                        inst = None
                        for dout in range(8):
                            ps = PS4[dout // 2][:, (dout % 2) * 512:(dout % 2 + 1) * 512]
                            inst = tensor.matmul(
                                ps, wslot[:, (i % 4) * 1024 + dout * 128:(i % 4) * 1024 + (dout + 1) * 128],
                                H[k][:, sub * 512:(sub + 1) * 512],
                                start=(k == 0), stop=(k == 31))
                        pe.inc(inst, 1)

            @blk.vector
            def _(vector):
                _fence(vector, F_)
                for sub in range(2):
                    vector.wait_ge(pe.h, pe0 + (sub + 1) * 32)
                    inst = None
                    for dout in range(8):
                        ps = PS4[dout // 2][:, (dout % 2) * 512:(dout % 2 + 1) * 512]
                        rs_ = resid[(dout, tb)][:, sub * 512:(sub + 1) * 512]
                        inst = vector.scalar_tensor_tensor(rs_, ps, hb2sb[:, dout:dout + 1],
                                                           rs_, ALU.add, ALU.add)
                    dv.inc(inst, 1)

    def emit_mm2b(tb, H, tag):
        with ExitStack() as s:
            wslot = s.enter_context(nc.sbuf_tensor(f"w2t{tag}", [128, 4 * 1024], BF16))
            otile = s.enter_context(nc.sbuf_tensor(f"ot{tag}", [128, 4 * 1024], F32))
            blk = s.enter_context(nc.Block())
            F_ = (pe.v, dv.v, ac.v)
            pe0, dv0, st0 = pe.v, dv.v, st.v
            pe_after = lambda tq, k: pe0 + tq * 33 + 1 + (k + 1)
            k_sem = {0: sy1, 2: sy2, 1: sc1, 3: sc2}
            sy_k = {}
            cnt = {0: sy1.v, 2: sy2.v, 1: sc1.v, 3: sc2.v}
            for tq in range(2):
                for k in range(32):
                    i = tq * 32 + k
                    cnt[i % 4] += 16
                    sy_k[(tq, k)] = (k_sem[i % 4], cnt[i % 4])
            st_o = [st0 + (j + 1) * 16 for j in range(8)]
            msE = [dv0 + j + 1 for j in range(8)]

            def _loads(eng, par):
                for tq in range(2):
                    for k in range(32):
                        i = tq * 32 + k
                        if par is not None and i % 2 != par:
                            continue
                        if i >= 4:
                            i2 = i - 4
                            eng.wait_ge(pe.h, pe_after(i2 // 32, i2 % 32))
                        v = k_sem[i % 4].inc(eng.dma_start(
                            out=wslot[:, (i % 4) * 1024:(i % 4 + 1) * 1024],
                            in_=wq2T_s[k * 128:(k + 1) * 128, 0:1024]), 16)
                        assert (k_sem[i % 4], v) == sy_k[(tq, k)]

            @blk.sync
            def _(sync):
                _fence(sync, F_)
                _loads(sync, 0)
                for sem in (sy1, sy2):
                    sync.wait_ge(sem.h, sem.v)

            @blk.scalar
            def _(scalar):
                _fence(scalar, F_)
                _loads(scalar, 1)
                for sem in (sc1, sc2):
                    scalar.wait_ge(sem.h, sem.v)

            @blk.tensor
            def _(tensor):
                _fence(tensor, F_)
                for tq in range(2):
                    if tq == 1:
                        tensor.wait_ge(dv.h, msE[3])
                    inst = None
                    for tcl in range(4):
                        tc = tq * 4 + tcl
                        ps = PS4[tcl]
                        for c in range(NC):
                            inst = tensor.matmul(ps[:, c * 128:(c + 1) * 128],
                                                 resid[(c, tb)][:, tc * 128:(tc + 1) * 128],
                                                 ident[:], start=(c in (0, 4)), stop=False)
                        tensor.matmul(ps[:, 0:512], ones_r[:], hb2row[0:1, 0:512],
                                      start=False, stop=False)
                        inst = tensor.matmul(ps[:, 512:1024], ones_r[:], hb2row[0:1, 512:1024],
                                             start=False, stop=False)
                    pe.inc(inst, 1)
                    for k in range(32):
                        tensor.wait_ge(sy_k[(tq, k)][0].h, sy_k[(tq, k)][1])
                        i = tq * 32 + k
                        inst = None
                        for tcl in range(4):
                            tc = tq * 4 + tcl
                            ps = PS4[tcl]
                            tensor.matmul(ps[:, 0:512], H[k][:, tc * 128:(tc + 1) * 128],
                                          wslot[:, (i % 4) * 1024:(i % 4) * 1024 + 512],
                                          start=False, stop=(k == 31))
                            inst = tensor.matmul(ps[:, 512:1024], H[k][:, tc * 128:(tc + 1) * 128],
                                                 wslot[:, (i % 4) * 1024 + 512:(i % 4 + 1) * 1024],
                                                 start=False, stop=(k == 31))
                        pe.inc(inst, 1)

            @blk.vector
            def _(vector):
                _fence(vector, F_)
                for tq in range(2):
                    vector.wait_ge(pe.h, pe0 + (tq + 1) * 33)
                    for tcl in range(4):
                        j = tq * 4 + tcl
                        if j >= 4:
                            vector.wait_ge(st.h, st_o[j - 4])
                        v = dv.inc(vector.tensor_copy(otile[:, (j % 4) * 1024:(j % 4 + 1) * 1024],
                                                      PS4[tcl][:]), 1)
                        assert v == msE[j]

            @blk.gpsimd
            def _(gpsimd):
                _fence(gpsimd, F_)
                for j in range(8):
                    gpsimd.wait_ge(dv.h, msE[j])
                    v = st.inc(gpsimd.dma_start(
                        out=y_out[tb * TB + j * 128:tb * TB + (j + 1) * 128, :],
                        in_=otile[:, (j % 4) * 1024:(j % 4 + 1) * 1024]), 16)
                    assert v == st_o[j]
                gpsimd.wait_ge(st.h, st.v)

    def debug_dump_resid():
        with nc.Block() as blk:
            F_ = (pe.v, dv.v, ac.v)
            @blk.sync
            def _(sync):
                _fence(sync, F_)
                for c in range(NC):
                    for tb in range(2):
                        st.inc(sync.dma_start(
                            out=y_out[(tb * 8 + c) * 128:(tb * 8 + c + 1) * 128, :],
                            in_=resid[(c, tb)][:]), 16)
                sync.wait_ge(st.h, st.v)
        build_program.resid_names = {k: t.name for k, t in resid.items()}
        return nc, ctx

    if stage == "x0":
        return debug_dump_resid()

    if stage == "rows":
        with nc.Block() as blk:
            F_ = (pe.v, dv.v, ac.v)
            @blk.gpsimd
            def _(gpsimd):
                _fence(gpsimd, F_)
                v = st.inc(gpsimd.dma_start(out=y_out[0:1, 0:16], in_=wsrow[:]), 16)
                v = st.inc(gpsimd.dma_start(out=y_out[1:129, 0:4], in_=rbc[:]), 16)
                v = st.inc(gpsimd.dma_start(out=y_out[129:257, 0:16], in_=gacp[:]), 16)
                gpsimd.wait_ge(st.h, st.v)
        build_program.resid_names = {}
        return nc, ctx

    if stage in ("wq1", "wq2"):
        src_t = wq1_s if stage == "wq1" else wq2_s
        with ExitStack() as dbg:
            tb16 = dbg.enter_context(nc.sbuf_tensor("dbg16", [128, 1024], BF16))
            tf32 = dbg.enter_context(nc.sbuf_tensor("dbg32", [128, 1024], F32))
            blk = dbg.enter_context(nc.Block())
            F_ = (pe.v, dv.v, ac.v)

            @blk.sync
            def _(sync):
                _fence(sync, F_)
                for i in range(16):
                    sync.wait_ge(dv.h, dv.v + 0)
                pass

            @blk.gpsimd
            def _(gpsimd):
                _fence(gpsimd, F_)
                for i in range(16):
                    if stage == "wq1":
                        gp1.inc(gpsimd.dma_start(out=tb16[:], in_=wq1_s[i * 128:(i + 1) * 128, 0:1024]), 16)
                    else:
                        gp1.inc(gpsimd.dma_start(out=tb16[:], in_=wq2_s[i * 64:(i + 1) * 64, 0:2048].rearrange("a b -> (a b)").rearrange("(p f) -> p f", p=128)), 16)
                    gpsimd.wait_ge(gp1.h, gp1.v)
                    gpsimd.tensor_copy(tf32[:], tb16[:])
                    v = st.inc(gpsimd.dma_start(out=y_out[i * 128:(i + 1) * 128, :], in_=tf32[:]), 16)
                    gpsimd.wait_ge(st.h, v)
                gpsimd.wait_ge(st.h, st.v)
        build_program.resid_names = {k: t.name for k, t in resid.items()}
        return nc, ctx

    for tb in range(2):
        with ExitStack() as sl:
            xln = [sl.enter_context(nc.sbuf_tensor(f"xln1_{tb}_{k}", [128, TB], BF16)) for k in range(NC)]
            H = [sl.enter_context(nc.sbuf_tensor(f"H1_{tb}_{k}", [128, TB], BF16)) for k in range(NF)]
            emit_ln(tb, xln, f"a{tb}")
            emit_mm1(tb, xln, H, f"a{tb}")
            emit_mm2a(tb, H, f"a{tb}")

    if stage == "s1":
        return debug_dump_resid()

    with ExitStack() as mam:
        wdtT = [mam.enter_context(nc.sbuf_tensor(f"wdtT{k}", [128, 1024], BF16)) for k in range(NC)]
        wxbT = [mam.enter_context(nc.sbuf_tensor(f"wxbT{k}", [128, 1024], BF16)) for k in range(NC)]

        with nc.Block() as blk:
            F_ = (pe.v, dv.v, ac.v)
            @blk.sync
            def _(sync):
                _fence(sync, F_)
                for k in range(NC):
                    sy3.inc(sync.dma_start(out=wdtT[k][:], in_=wdt_s[0:1024, k * 128:(k + 1) * 128],
                                           transpose=True), 16)
                    sy3.inc(sync.dma_start(out=wxbT[k][:], in_=wx_s[0:1024, k * 128:(k + 1) * 128],
                                           transpose=True), 16)
                sync.wait_ge(sy3.h, sy3.v)

        for s_ in range(2):
            with ExitStack() as sscope:
                ksb = [sscope.enter_context(nc.sbuf_tensor(f"k{s_}_{c}", [128, TB], BF16)) for c in range(NC)]
                invdt_bc = sscope.enter_context(nc.sbuf_tensor(f"ivd{s_}", [128, TB], F32))
                invx_bc = sscope.enter_context(nc.sbuf_tensor(f"ivx{s_}", [128, TB], F32))

                with ExitStack() as b4:
                    sq = b4.enter_context(nc.sbuf_tensor(f"m4sq{s_}", [128, 2 * TB], BF16))
                    rowb = b4.enter_context(nc.sbuf_tensor(f"m4row{s_}", [1, 4 * TB], F32))
                    xcp = b4.enter_context(nc.sbuf_tensor(f"m4xc{s_}", [128, 2 * TB], F32))
                    trs = [b4.enter_context(nc.sbuf_tensor(f"m4tr{s_}_{i}", [128, TB], F32)) for i in range(7)]
                    cmax = b4.enter_context(nc.sbuf_tensor(f"m4c{s_}", [128, 8], F32))
                    ct8 = b4.enter_context(nc.sbuf_tensor(f"m4ct{s_}", [8, 128], F32))
                    qi = b4.enter_context(nc.sbuf_tensor(f"m4qi{s_}", [128, TB], F32))
                    blk = b4.enter_context(nc.Block())
                    F_ = (pe.v, dv.v, ac.v)
                    pe0, dv0, ac0 = pe.v, dv.v, ac.v
                    gp1_0, sy3_0 = gp1.v, sy3.v
                    R = lambda i, o=0, w=TB: rowb[0:1, i * TB + o:i * TB + o + w]
                    ms = {}

                    @blk.vector
                    def _(vector):
                        _fence(vector, F_)
                        vector.wait_ge(pe.h, pe0 + 8)
                        v = dv.inc(vector.tensor_copy(R(0), pC[0:1, 0:TB]), 1)
                        v = dv.inc(vector.tensor_copy(R(1), pD[0:1, 0:TB]), 1)
                        vector.wait_ge(dv.h, v)
                        v = dv.inc(vector.tensor_scalar(R(2), R(0), INV1024, None, ALU.mult), 1)
                        v = dv.inc(vector.tensor_scalar(R(3), R(1), INV1024, None, ALU.mult), 1)
                        vector.wait_ge(dv.h, v)
                        v = dv.inc(vector.tensor_tensor(R(1), R(2), R(2), ALU.mult), 1)
                        vector.wait_ge(dv.h, v)
                        v = dv.inc(vector.tensor_tensor(R(0), R(3), R(1), ALU.subtract), 1)
                        vector.wait_ge(dv.h, v)
                        ms["vareps"] = dv.inc(vector.tensor_scalar(R(1), R(0), 1e-5, None, ALU.add), 1)
                        vector.wait_ge(pe.h, pe0 + 9)
                        for i in range(4):
                            o = 0 if i % 2 == 0 else 2
                            v = dv.inc(vector.tensor_tensor(xcp[:, 0:TB], resid[(2 * i, s_)][:],
                                                            pC[:, 0:TB], ALU.subtract), 1)
                            v = dv.inc(vector.tensor_tensor(xcp[:, TB:2 * TB], resid[(2 * i + 1, s_)][:],
                                                            pC[:, 0:TB], ALU.subtract), 1)
                            vector.wait_ge(dv.h, v)
                            vector.tensor_tensor(trs[o][:], xcp[:, 0:TB], xcp[:, TB:2 * TB], ALU.max)
                            dv.inc(vector.tensor_tensor(trs[o + 1][:], xcp[:, 0:TB], xcp[:, TB:2 * TB], ALU.min), 1)
                            if i == 1:
                                vector.tensor_tensor(trs[4][:], trs[0][:], trs[2][:], ALU.max)
                                vector.tensor_tensor(trs[5][:], trs[1][:], trs[3][:], ALU.min)
                        v = dv.inc(vector.tensor_tensor(trs[6][:], trs[0][:], trs[2][:], ALU.max), 1)
                        v = dv.inc(vector.tensor_tensor(trs[0][:], trs[1][:], trs[3][:], ALU.min), 1)
                        vector.wait_ge(dv.h, v)
                        v = dv.inc(vector.tensor_tensor(trs[1][:], trs[4][:], trs[6][:], ALU.max), 1)
                        v = dv.inc(vector.tensor_tensor(trs[2][:], trs[5][:], trs[0][:], ALU.min), 1)
                        vector.wait_ge(dv.h, v)
                        v = dv.inc(vector.tensor_scalar(trs[3][:], trs[2][:], -1.0, None, ALU.mult), 1)
                        vector.wait_ge(dv.h, v)
                        ms["m1"] = dv.inc(vector.tensor_tensor(trs[4][:], trs[1][:], trs[3][:], ALU.max), 1)
                        vector.wait_ge(pe.h, pe0 + 10)
                        inst = None
                        for jj in range(8):
                            inst = vector.tensor_reduce(cmax[:, jj:jj + 1], pD[:, jj * 128:(jj + 1) * 128],
                                                        AX, ALU.max)
                        ms["cmax"] = dv.inc(inst, 1)
                        vector.wait_ge(pe.h, pe0 + 11)
                        ms["ct8"] = dv.inc(vector.tensor_copy(ct8[:], pD[0:8, 0:128]), 1)
                        vector.wait_ge(sy3.h, sy3_0 + 16)
                        vector.wait_ge(ac.h, ac0 + 10)
                        v = dv.inc(vector.tensor_tensor(R(3), R(0), R(1), ALU.mult), 1)
                        vector.wait_ge(dv.h, v)
                        v = dv.inc(vector.tensor_scalar(R(0), R(3), 1e-5, None, ALU.max), 1)
                        vector.wait_ge(dv.h, v)
                        v = dv.inc(vector.reciprocal(R(3), R(0)), 1)
                        vector.wait_ge(dv.h, v)
                        ms["rs"] = dv.inc(vector.scalar_tensor_tensor(R(2), R(1), 127.0, R(3),
                                                                      ALU.mult, ALU.mult), 1)
                        vector.wait_ge(dv.h, ms["rs"])
                        v = dv.inc(vector.tensor_scalar(R(1), R(0), wsrow[0:1, 7:8], INV127,
                                                        ALU.mult, ALU.mult), 1)
                        ms["inv"] = dv.inc(vector.tensor_scalar(R(3), R(0), wsrow[0:1, 6:7], INV127,
                                                                ALU.mult, ALU.mult), 1)
                        vector.wait_ge(pe.h, pe0 + 12)
                        for c in range(NC):
                            v = dv.inc(vector.tensor_tensor(xcp[:, 0:TB], resid[(c, s_)][:],
                                                            pC[:, 0:TB], ALU.subtract), 1)
                            vector.wait_ge(dv.h, v)
                            v = dv.inc(vector.tensor_tensor(qi[:], xcp[:, 0:TB], pD[:, 0:TB], ALU.mult), 1)
                            vector.wait_ge(dv.h, v)
                            v = dv.inc(vector.tensor_scalar(qi[:], qi[:], MAGIC, MAGIC, ALU.add, ALU.subtract), 1)
                            vector.wait_ge(dv.h, v)
                            dv.inc(vector.tensor_scalar(ksb[c][:], qi[:], 127.0, -128.0,
                                                        ALU.min, ALU.max), 1)
                        ms["kdone"] = dv.v
                        vector.wait_ge(pe.h, pe0 + 13)
                        dv.inc(vector.tensor_copy(invdt_bc[:], pC[:, 0:TB]), 1)
                        vector.wait_ge(pe.h, pe0 + 14)
                        dv.inc(vector.tensor_copy(invx_bc[:], pD[:, 0:TB]), 1)

                    @blk.scalar
                    def _(scalar):
                        _fence(scalar, F_)
                        for c in range(NC):
                            if c >= 2:
                                scalar.wait_ge(pe.h, pe0 + c - 1)
                            ac.inc(scalar.activation(sq[:, (c % 2) * TB:(c % 2 + 1) * TB],
                                                     resid[(c, s_)][:], AF.Square), 1)
                        scalar.wait_ge(dv.h, ms["vareps"])
                        v = ac.inc(scalar.activation(R(3), R(1), AF.Ln), 1)
                        scalar.wait_ge(ac.h, v)
                        v = ac.inc(scalar.activation(R(1), R(3), AF.Exp, scale=-0.5), 1)
                        assert v == ac0 + 10

                    @blk.tensor
                    def _(tensor):
                        _fence(tensor, F_)
                        for c in range(NC):
                            tensor.wait_ge(ac.h, ac0 + c + 1)
                            rs_ = resid[(c, s_)]
                            ss = sq[:, (c % 2) * TB:(c % 2 + 1) * TB]
                            tensor.matmul(pC[0:1, 0:512], ones_cf[:], rs_[:, 0:512], start=(c == 0), stop=(c == 7))
                            tensor.matmul(pC[0:1, 512:1024], ones_cf[:], rs_[:, 512:1024], start=(c == 0), stop=(c == 7))
                            tensor.matmul(pD[0:1, 0:512], ones_cb[:], ss[:, 0:512], start=(c == 0), stop=(c == 7))
                            inst = tensor.matmul(pD[0:1, 512:1024], ones_cb[:], ss[:, 512:1024], start=(c == 0), stop=(c == 7))
                            pe.inc(inst, 1)
                        tensor.wait_ge(dv.h, ms["vareps"])
                        tensor.matmul(pC[:, 0:512], ones_r[:], R(2, 0, 512), start=True, stop=True)
                        inst = tensor.matmul(pC[:, 512:1024], ones_r[:], R(2, 512, 512), start=True, stop=True)
                        pe.inc(inst, 1)
                        tensor.wait_ge(dv.h, ms["m1"])
                        inst = None
                        for jj in range(8):
                            inst = tensor.matmul(pD[:, jj * 128:(jj + 1) * 128],
                                                 trs[4][:, jj * 128:(jj + 1) * 128], ident[:],
                                                 start=(jj in (0, 4)), stop=(jj in (3, 7)))
                        pe.inc(inst, 1)
                        tensor.wait_ge(dv.h, ms["cmax"])
                        pe.inc(tensor.matmul(pD[0:8, 0:128], cmax[:], ident[:], start=True, stop=True), 1)
                        tensor.wait_ge(dv.h, ms["rs"])
                        tensor.wait_ge(dv.h, ms["ct8"])
                        tensor.matmul(pD[:, 0:512], ones_r[:], R(2, 0, 512), start=True, stop=True)
                        inst = tensor.matmul(pD[:, 512:1024], ones_r[:], R(2, 512, 512), start=True, stop=True)
                        pe.inc(inst, 1)
                        tensor.wait_ge(dv.h, ms["kdone"])
                        tensor.matmul(pC[:, 0:512], ones_r[:], R(1, 0, 512), start=True, stop=True)
                        inst = tensor.matmul(pC[:, 512:1024], ones_r[:], R(1, 512, 512), start=True, stop=True)
                        pe.inc(inst, 1)
                        tensor.matmul(pD[:, 0:512], ones_r[:], R(3, 0, 512), start=True, stop=True)
                        inst = tensor.matmul(pD[:, 512:1024], ones_r[:], R(3, 512, 512), start=True, stop=True)
                        pe.inc(inst, 1)

                    @blk.gpsimd
                    def _(gpsimd):
                        _fence(gpsimd, F_)
                        gpsimd.wait_ge(dv.h, ms["ct8"])
                        gp1.inc(gpsimd.dma_start(out=bounce, in_=ct8[:]), 16)
                        gpsimd.wait_ge(gp1.h, gp1.v)

                    @blk.sync
                    def _(sync):
                        _fence(sync, F_)
                        sync.wait_ge(gp1.h, gp1_0 + 16)
                        sy3.inc(sync.dma_start(out=rowb[0:1, 0:TB],
                                               in_=bounce.rearrange("a b -> (a b)")[None, :]), 16)
                        sync.wait_ge(sy3.h, sy3.v)

                with ExitStack() as b5:
                    wxc = b5.enter_context(nc.sbuf_tensor(f"b5wxc{s_}", [128, 2 * 1024], BF16))
                    dt0 = b5.enter_context(nc.sbuf_tensor(f"b5dt0{s_}", [128, 512], F32))
                    b0t = b5.enter_context(nc.sbuf_tensor(f"b5b0{s_}", [128, 512], F32))
                    c0t = b5.enter_context(nc.sbuf_tensor(f"b5c0{s_}", [128, 2 * 512], F32))
                    zt = b5.enter_context(nc.sbuf_tensor(f"b5z{s_}", [128, 512], F32))
                    btt = b5.enter_context(nc.sbuf_tensor(f"b5bt{s_}", [128, 512], F32))
                    amt = b5.enter_context(nc.sbuf_tensor(f"b5am{s_}", [128, 512], F32))
                    a1t = b5.enter_context(nc.sbuf_tensor(f"b5a1{s_}", [128, 512], F32))
                    e1t = b5.enter_context(nc.sbuf_tensor(f"b5e1{s_}", [128, 512], F32))
                    elt = b5.enter_context(nc.sbuf_tensor(f"b5el{s_}", [128, 512], F32))
                    alt = b5.enter_context(nc.sbuf_tensor(f"b5al{s_}", [128, 512], F32))
                    ppt = b5.enter_context(nc.sbuf_tensor(f"b5pp{s_}", [128, 2 * 512], F32))
                    rsh = b5.enter_context(nc.sbuf_tensor(f"b5rsh{s_}", [128, 512], F32))
                    dbs = b5.enter_context(nc.sbuf_tensor(f"b5db{s_}", [128, 512], F32))
                    ut = b5.enter_context(nc.sbuf_tensor(f"b5u{s_}", [128, 2 * 512], F32))
                    uct = b5.enter_context(nc.sbuf_tensor(f"b5uc{s_}", [128, 2 * 512], F32))
                    blk = b5.enter_context(nc.Block())
                    F_ = (pe.v, dv.v, ac.v)
                    pe0, dv0, ac0 = pe.v, dv.v, ac.v
                    gp1_0, gp2_0 = gp1.v, gp2.v
                    NA = 4
                    iters = [(c, h) for c in range(NC) for h in range(2)]
                    c_sem = [sy1, sy2]
                    sy_c = [(c_sem[c % 2], c_sem[c % 2].v + (c // 2 + 1) * 128) for c in range(NC)]
                    i1, i2, i3 = [], [], []

                    @blk.vector
                    def _(vector):
                        _fence(vector, F_)
                        for q, (c, h) in enumerate(iters):
                            hs = slice(h * 512, (h + 1) * 512)
                            sl = slice((q % 2) * 512, (q % 2 + 1) * 512)
                            vector.wait_ge(pe.h, pe0 + q * 3 + 1)
                            if q >= 1:
                                vector.wait_ge(ac.h, ac0 + (q - 1) * NA + 1)
                            vector.tensor_tensor(dt0[:], pA[:, hs], invdt_bc[:, hs], ALU.mult)
                            vector.wait_ge(pe.h, pe0 + q * 3 + 2)
                            vector.tensor_tensor(b0t[:], pB[:, hs], invx_bc[:, hs], ALU.mult)
                            vector.wait_ge(pe.h, pe0 + q * 3 + 3)
                            if q >= 2:
                                vector.wait_ge(gp1.h, gp1_0 + (q - 1) * 16)
                            inst = vector.tensor_tensor(c0t[:, sl], pC[:, hs], invx_bc[:, hs], ALU.mult)
                            i1.append(dv.inc(inst, 1))
                            vector.tensor_scalar(btt[:], b0t[:], bxsb[:, c:c + 1], 5.0, ALU.add, ALU.min)
                            vector.tensor_scalar(zt[:], dt0[:], bdtsb[:, c:c + 1], None, ALU.add)
                            vector.tensor_scalar(b0t[:], btt[:], -5.0, None, ALU.max)
                            vector.wait_ge(ac.h, ac0 + q * NA + 3)
                            v = dv.inc(vector.scalar_tensor_tensor(btt[:], zt[:], 0.0, elt[:],
                                                                   ALU.max, ALU.add), 1)
                            vector.wait_ge(dv.h, v)
                            v = dv.inc(vector.tensor_scalar(zt[:], btt[:], 1.0, 0.001, ALU.min, ALU.max), 1)
                            vector.wait_ge(dv.h, v)
                            inst = vector.tensor_scalar(amt[:], zt[:], asb[:, c:c + 1], -10.0,
                                                        ALU.mult, ALU.max)
                            i2.append(dv.inc(inst, 1))
                            vector.wait_ge(ac.h, ac0 + q * NA + 4)
                            if q >= 2:
                                vector.wait_ge(gp2.h, gp2_0 + (q - 1) * 32)
                            v = dv.inc(vector.scalar_tensor_tensor(dbs[:], zt[:], S64, b0t[:],
                                                                   ALU.mult, ALU.mult), 1)
                            if h == 0:
                                initP = chainP[:, c:c + 1]
                                initS = chainS[:, c:c + 1]
                            else:
                                initP = ppt[:, ((q - 1) % 2) * 512 + 511:((q - 1) % 2) * 512 + 512]
                                initS = uct[:, ((q - 1) % 2) * 512 + 511:((q - 1) % 2) * 512 + 512]
                            v = dv.inc(vector.tensor_tensor_scan(ppt[:, sl], alt[:], zeros_h[:],
                                                                 initP, ALU.mult, ALU.add), 1)
                            vector.wait_ge(dv.h, v)
                            vector.reciprocal(rsh[:, 0:1], initP)
                            v = dv.inc(vector.reciprocal(rsh[:, 1:512],
                                                         ppt[:, (q % 2) * 512:(q % 2) * 512 + 511]), 1)
                            vector.wait_ge(dv.h, v)
                            v = dv.inc(vector.tensor_tensor(ut[:, sl], dbs[:], rsh[:], ALU.mult), 1)
                            vector.wait_ge(dv.h, v)
                            inst = vector.tensor_tensor_scan(uct[:, sl], ut[:, sl], zeros_h[:],
                                                             initS, ALU.add, ALU.add)
                            v = dv.inc(inst, 1)
                            if h == 1:
                                vector.wait_ge(dv.h, v)
                                if s_ == 0:
                                    vector.tensor_copy(chainP[:, c:c + 1],
                                                       ppt[:, (q % 2) * 512 + 511:(q % 2) * 512 + 512])
                                    v = dv.inc(vector.tensor_copy(chainS[:, c:c + 1],
                                                                  uct[:, (q % 2) * 512 + 511:(q % 2) * 512 + 512]), 1)
                                else:
                                    vector.tensor_copy(ends[:, c:c + 1],
                                                       ppt[:, (q % 2) * 512 + 511:(q % 2) * 512 + 512])
                                    v = dv.inc(vector.tensor_copy(ends[:, 8 + c:9 + c],
                                                                  uct[:, (q % 2) * 512 + 511:(q % 2) * 512 + 512]), 1)
                            i3.append(v)

                    @blk.scalar
                    def _(scalar):
                        _fence(scalar, F_)
                        for q, (c, h) in enumerate(iters):
                            scalar.wait_ge(dv.h, i1[q])
                            v = ac.inc(scalar.activation(a1t[:], dt0[:], AF.Abs, bias=bdtsb[:, c:c + 1]), 1)
                            scalar.wait_ge(ac.h, v)
                            v = ac.inc(scalar.activation(e1t[:], a1t[:], AF.Exp, scale=-1.0), 1)
                            scalar.wait_ge(ac.h, v)
                            ac.inc(scalar.activation(elt[:], e1t[:], AF.Ln, bias=1.0), 1)
                            scalar.wait_ge(dv.h, i2[q])
                            if q >= 1:
                                scalar.wait_ge(dv.h, i3[q - 1])
                            ac.inc(scalar.activation(alt[:], amt[:], AF.Exp), 1)

                    @blk.tensor
                    def _(tensor):
                        _fence(tensor, F_)
                        for q, (c, h) in enumerate(iters):
                            if q >= 1:
                                tensor.wait_ge(dv.h, i1[q - 1])
                            if h == 0:
                                tensor.wait_ge(sy_c[c][0].h, sy_c[c][1])
                            hs = slice(h * 512, (h + 1) * 512)
                            inst = None
                            for k in range(NC):
                                inst = tensor.matmul(pA[:, hs], wdtT[k][:, c * 128:(c + 1) * 128],
                                                     ksb[k][:, hs], start=(k == 0), stop=(k == 7))
                            pe.inc(inst, 1)
                            inst = None
                            for k in range(NC):
                                inst = tensor.matmul(pB[:, hs], wxbT[k][:, c * 128:(c + 1) * 128],
                                                     ksb[k][:, hs], start=(k == 0), stop=(k == 7))
                            pe.inc(inst, 1)
                            inst = None
                            for k in range(NC):
                                inst = tensor.matmul(pC[:, hs],
                                                     wxc[:, (c % 2) * 1024 + k * 128:(c % 2) * 1024 + (k + 1) * 128],
                                                     ksb[k][:, hs], start=(k == 0), stop=(k == 7))
                            pe.inc(inst, 1)

                    @blk.gpsimd
                    def _(gpsimd):
                        _fence(gpsimd, F_)
                        for q, (c, h) in enumerate(iters):
                            row0 = (s_ * 8 + c) * 128
                            hs = slice(h * 512, (h + 1) * 512)
                            sl = slice((q % 2) * 512, (q % 2 + 1) * 512)
                            gpsimd.wait_ge(dv.h, i1[q])
                            gp1.inc(gpsimd.dma_start(out=c_sp[row0:row0 + 128, hs], in_=c0t[:, sl]), 16)
                            gpsimd.wait_ge(dv.h, i3[q])
                            gp2.inc(gpsimd.dma_start(out=u_sp[row0:row0 + 128, hs], in_=ut[:, sl]), 16)
                            gp2.inc(gpsimd.dma_start(out=p_sp[row0:row0 + 128, hs], in_=ppt[:, sl]), 16)
                        gpsimd.wait_ge(gp1.h, gp1.v)
                        gpsimd.wait_ge(gp2.h, gp2.v)

                    @blk.sync
                    def _(sync):
                        _fence(sync, F_)
                        for c in range(NC):
                            if c >= 2:
                                sync.wait_ge(pe.h, pe0 + 6 * (c - 1))
                            v = None
                            for k in range(NC):
                                v = c_sem[c % 2].inc(sync.dma_start(
                                    out=wxc[:, (c % 2) * 1024 + k * 128:(c % 2) * 1024 + (k + 1) * 128],
                                    in_=wx_s[1024 + c * 128:1024 + (c + 1) * 128, k * 128:(k + 1) * 128],
                                    transpose=True), 16)
                            assert (c_sem[c % 2], v) == sy_c[c]
                        for sem in (sy1, sy2):
                            sync.wait_ge(sem.h, sem.v)

        with ExitStack() as b6:
            gath = b6.enter_context(nc.sbuf_tensor("gath", [128, 128], F32))
            part = b6.enter_context(nc.sbuf_tensor("part", [128, 16], F32))
            ptmp = b6.enter_context(nc.sbuf_tensor("ptmp", [128, 96], F32))
            blk = b6.enter_context(nc.Block())
            F_ = (pe.v, dv.v, ac.v)
            gp1_0, gp2_0, dv0 = gp1.v, gp2.v, dv.v

            @blk.gpsimd
            def _(gpsimd):
                _fence(gpsimd, F_)
                gp1.inc(gpsimd.dma_start(out=cc_in.ap(), in_=ends[:]), 16)
                gpsimd.wait_ge(gp1.h, gp1_0 + 16)
                v = ccs.inc(gpsimd.collective_compute(
                    "AllGather", ALU.bypass, replica_groups=[list(range(8))],
                    ins=[cc_in.ap().opt()], outs=[cc_out.ap().opt()]), 1)
                gpsimd.wait_ge(ccs.h, v)
                for r in range(8):
                    gp2.inc(gpsimd.dma_start(out=gath[:, r * 16:(r + 1) * 16],
                                             in_=cc_out.ap()[r * 128:(r + 1) * 128, :]), 16)
                gpsimd.wait_ge(gp2.h, gp2.v)

            @blk.vector
            def _(vector):
                _fence(vector, F_)
                vector.wait_ge(gp2.h, gp2_0 + 8 * 16)
                for i in range(4):
                    dv.inc(vector.select(ptmp[:, i * 16:(i + 1) * 16], sbits[:, 0:16],
                                         gath[:, (2 * i + 1) * 16:(2 * i + 2) * 16],
                                         gath[:, (2 * i) * 16:(2 * i + 1) * 16]), 1)
                vector.wait_ge(dv.h, dv.v)
                for i in range(2):
                    dv.inc(vector.select(ptmp[:, 64 + i * 16:64 + (i + 1) * 16], sbits[:, 16:32],
                                         ptmp[:, (2 * i + 1) * 16:(2 * i + 2) * 16],
                                         ptmp[:, (2 * i) * 16:(2 * i + 1) * 16]), 1)
                vector.wait_ge(dv.h, dv.v)
                v = dv.inc(vector.select(part[:], sbits[:, 32:48], ptmp[:, 80:96], ptmp[:, 64:80]), 1)
                vector.wait_ge(dv.h, v)
                for c in range(NC):
                    v = dv.inc(vector.select(ptmp[:, 0:1], musb[:], part[:, c:c + 1], bigcol[:]), 1)
                    vector.wait_ge(dv.h, v)
                    v = dv.inc(vector.reciprocal(ptmp[:, 1:2], ptmp[:, 0:1]), 1)
                    vector.wait_ge(dv.h, v)
                    dv.inc(vector.tensor_scalar(r1s_t[:, c:c + 1], ptmp[:, 1:2], S64, None, ALU.mult), 1)
                    dv.inc(vector.tensor_scalar(q2_t[:, c:c + 1], ptmp[:, 0:1], S64I, None, ALU.mult), 1)
                    dv.inc(vector.select(chainS2[:, c:c + 1], musb[:], part[:, 8 + c:9 + c], zcol[:]), 1)
                dv.inc(vector.tensor_copy(ptmp[:, 0:1], ptmp[:, 0:1]), 1)

        with ExitStack() as b7:
            c0r = b7.enter_context(nc.sbuf_tensor("b7c0", [128, 2 * 512], F32))
            ur = b7.enter_context(nc.sbuf_tensor("b7u", [128, 2 * 512], F32))
            ppr = b7.enter_context(nc.sbuf_tensor("b7pp", [128, 2 * 512], F32))
            ctt = b7.enter_context(nc.sbuf_tensor("b7ct", [128, 512], F32))
            ct2 = b7.enter_context(nc.sbuf_tensor("b7ct2", [128, 512], F32))
            e2t = b7.enter_context(nc.sbuf_tensor("b7e2", [128, 512], F32))
            tht = b7.enter_context(nc.sbuf_tensor("b7th", [128, 512], F32))
            gt = b7.enter_context(nc.sbuf_tensor("b7g", [128, 512], F32))
            cst = b7.enter_context(nc.sbuf_tensor("b7cs", [128, 2 * 512], F32))
            q1t = b7.enter_context(nc.sbuf_tensor("b7q1", [128, 512], F32))
            tqt = b7.enter_context(nc.sbuf_tensor("b7tq", [128, 512], F32))
            htt = b7.enter_context(nc.sbuf_tensor("b7h", [128, 512], F32))
            yt = b7.enter_context(nc.sbuf_tensor("b7y", [128, 512], F32))
            ryt = b7.enter_context(nc.sbuf_tensor("b7ry", [128, 512], F32))
            mkt = b7.enter_context(nc.sbuf_tensor("b7mk", [128, 512], I32))
            ayt = b7.enter_context(nc.sbuf_tensor("b7ay", [128, 512], F32))
            l1t = b7.enter_context(nc.sbuf_tensor("b7l1", [128, 512], F32))
            emt = b7.enter_context(nc.sbuf_tensor("b7em", [128, 512], F32))
            blk = b7.enter_context(nc.Block())
            F_ = (pe.v, dv.v, ac.v)
            dv0, ac0 = dv.v, ac.v
            NA7 = 4
            iters = [(si, c, h) for si in range(2) for c in range(NC) for h in range(2)]
            r_sem = [sy1, sy2]
            sy_r = [(r_sem[q % 2], r_sem[q % 2].v + (q // 2 + 1) * 48) for q in range(32)]
            i1, i2, i3 = [], [], []

            @blk.vector
            def _(vector):
                _fence(vector, F_)
                for q, (si, c, h) in enumerate(iters):
                    sl = slice((q % 2) * 512, (q % 2 + 1) * 512)
                    vector.wait_ge(sy_r[q][0].h, sy_r[q][1])
                    if q >= 1:
                        vector.wait_ge(ac.h, ac0 + (q - 1) * NA7 + 1)
                    vector.tensor_scalar(ctt[:], c0r[:, sl], bxsb[:, 8 + c:9 + c], 5.0,
                                         ALU.add, ALU.min)
                    vector.tensor_scalar(gt[:], ur[:, sl], r1s_t[:, c:c + 1], None, ALU.mult)
                    inst = vector.tensor_scalar(ct2[:], ctt[:], -5.0, None, ALU.max)
                    i1.append(dv.inc(inst, 1))
                    vector.wait_ge(ac.h, ac0 + q * NA7 + 1)
                    vector.tensor_scalar(q1t[:], ppr[:, sl], S64I, None, ALU.mult)
                    if h == 0:
                        initS = chainS2[:, c:c + 1]
                    else:
                        initS = cst[:, ((q - 1) % 2) * 512 + 511:((q - 1) % 2) * 512 + 512]
                    v = dv.inc(vector.tensor_tensor_scan(cst[:, sl], gt[:], zeros_h[:],
                                                         initS, ALU.add, ALU.add), 1)
                    vector.wait_ge(dv.h, v)
                    vector.wait_ge(ac.h, ac0 + q * NA7 + 3)
                    vector.tensor_scalar(tht[:], emt[:], -2.0, 1.0, ALU.mult, ALU.add)
                    v = dv.inc(vector.tensor_tensor(tqt[:], q1t[:], cst[:, sl], ALU.mult), 1)
                    vector.wait_ge(dv.h, v)
                    v = dv.inc(vector.tensor_scalar(htt[:], tqt[:], q2_t[:, c:c + 1], None, ALU.mult), 1)
                    vector.wait_ge(dv.h, v)
                    inst = vector.tensor_tensor(yt[:], htt[:], tht[:], ALU.mult)
                    i2.append(dv.inc(inst, 1))
                    vector.wait_ge(dv.h, i2[q])
                    rs_ = resid[(c, si)][:, h * 512:(h + 1) * 512]
                    sp_ = spare[0][:, h * 512:(h + 1) * 512]
                    vector.tensor_tensor(ryt[:], rs_, yt[:], ALU.add)
                    if si == 0 and h == 1:
                        vector.tensor_copy(chainS2[:, c:c + 1],
                                           cst[:, (q % 2) * 512 + 511:(q % 2) * 512 + 512])
                    vector.wait_ge(ac.h, ac0 + q * NA7 + 4)
                    v = dv.inc(vector.tensor_scalar(mkt[:], ayt[:], 3.4028234663852886e38,
                                                    None, ALU.is_le), 1)
                    vector.wait_ge(dv.h, v)
                    inst = vector.select(sp_, mkt[:], ryt[:], rs_)
                    i3.append(dv.inc(inst, 1))
                    if h == 1:
                        old = resid[(c, si)]
                        resid[(c, si)] = spare[0]
                        spare[0] = old

            @blk.sync
            def _(sync):
                _fence(sync, F_)
                for q, (si, c, h) in enumerate(iters):
                    row0 = (si * 8 + c) * 128
                    hs = slice(h * 512, (h + 1) * 512)
                    if q >= 2:
                        sync.wait_ge(dv.h, i2[q - 2])
                    sl = slice((q % 2) * 512, (q % 2 + 1) * 512)
                    sem = r_sem[q % 2]
                    sem.inc(sync.dma_start(out=c0r[:, sl], in_=c_sp[row0:row0 + 128, hs]), 16)
                    sem.inc(sync.dma_start(out=ur[:, sl], in_=u_sp[row0:row0 + 128, hs]), 16)
                    v = sem.inc(sync.dma_start(out=ppr[:, sl], in_=p_sp[row0:row0 + 128, hs]), 16)
                    assert (sem, v) == sy_r[q]
                for sem in (sy1, sy2):
                    sync.wait_ge(sem.h, sem.v)

            @blk.scalar
            def _(scalar):
                _fence(scalar, F_)
                for q, (si, c, h) in enumerate(iters):
                    scalar.wait_ge(dv.h, i1[q])
                    v = ac.inc(scalar.activation(e2t[:], ct2[:], AF.Exp, scale=2.0), 1)
                    scalar.wait_ge(ac.h, v)
                    v = ac.inc(scalar.activation(l1t[:], e2t[:], AF.Ln, bias=1.0), 1)
                    scalar.wait_ge(ac.h, v)
                    ac.inc(scalar.activation(emt[:], l1t[:], AF.Exp, scale=-1.0), 1)
                    scalar.wait_ge(dv.h, i2[q])
                    ac.inc(scalar.activation(ayt[:], yt[:], AF.Abs), 1)

    if stage == "s2":
        return debug_dump_resid()

    for tb in range(2):
        with ExitStack() as sl:
            xln = [sl.enter_context(nc.sbuf_tensor(f"xln3_{tb}_{k}", [128, TB], BF16)) for k in range(NC)]
            H = [sl.enter_context(nc.sbuf_tensor(f"H3_{tb}_{k}", [128, TB], BF16)) for k in range(NF)]
            emit_ln(tb, xln, f"c{tb}")
            emit_mm1(tb, xln, H, f"c{tb}")
            emit_mm2b(tb, H, f"c{tb}")

    build_program.resid_names = {k: t.name for k, t in resid.items()}
    return nc, ctx


def kernel(**inputs):
    x = np.asarray(inputs["L0_attn_x"], dtype=np.float32)
    W_x = np.asarray(inputs["L0_attn_W_x"], dtype=np.float32)
    b_x = np.asarray(inputs["L0_attn_b_x"], dtype=np.float32)
    W_dt = np.asarray(inputs["L0_attn_W_dt"], dtype=np.float32)
    b_dt = np.asarray(inputs["L0_attn_b_dt"], dtype=np.float32)
    A_log = np.asarray(inputs["L0_attn_A_log"], dtype=np.float32)
    W1 = np.asarray(inputs["L0_ffn_W1"], dtype=np.float32)
    b1 = np.asarray(inputs["L0_ffn_b1"], dtype=np.float32)
    W2 = np.asarray(inputs["L0_ffn_W2"], dtype=np.float32)
    b2 = np.asarray(inputs["L0_ffn_b2"], dtype=np.float32)

    nc, ctx = build_program()

    ident = np.eye(128, dtype=np.float32)
    in_maps = []
    for core in range(8):
        b, j = core // 2, core % 2
        partner = (core - 1) % 8
        mu = np.full((128, 1), 1 if j == 1 else 0, np.int32)
        sb = np.zeros((128, 48), np.int32)
        for bit in range(3):
            sb[:, bit * 16:(bit + 1) * 16] = (partner >> bit) & 1
        gsl = np.concatenate([
            W1[core * 512:(core + 1) * 512, :],
            W2[core * 128:(core + 1) * 128, :].reshape(512, 1024),
            W_x[core * 256:(core + 1) * 256, :],
            W_dt[core * 128:(core + 1) * 128, :],
        ], axis=0)
        in_maps.append({
            "x_in": np.ascontiguousarray(x[b, j * TPC:(j + 1) * TPC, :]),
            "w1_in": W1, "b1_in": b1, "w2_in": W2, "b2_in": b2,
            "wx_in": W_x, "bx_in": b_x, "wdt_in": W_dt, "bdt_in": b_dt,
            "alog_in": A_log, "ident_in": ident, "mu_in": mu, "sbit_in": sb,
            "gsl_in": np.ascontiguousarray(gsl),
        })

    res = run_bass_kernel_spmd(nc, in_maps, core_ids=list(range(8)))

    out = np.empty((4, 4096, 1024), dtype=np.float32)
    for core in range(8):
        b, j = core // 2, core % 2
        out[b, j * TPC:(j + 1) * TPC, :] = res.results[core]["y_out"]
    return out


# version 23
# speedup vs baseline: 1.0179x; 1.0179x over previous
import os

import numpy as np
from contextlib import ExitStack

if os.environ.get("BASS_SIM_FP64") == "1":
    from concourse import bass_interp as _bi

    _orig_coresim_init = _bi.CoreSim.__init__

    def _coresim_init_allow_overflow(self, *a, **kw):
        kw.setdefault("require_finite", False)
        kw.setdefault("require_nnan", False)
        return _orig_coresim_init(self, *a, **kw)

    _bi.CoreSim.__init__ = _coresim_init_allow_overflow

import concourse.bass as bass
import concourse.mybir as mybir
from concourse.bass_utils import run_bass_kernel_spmd

F32 = mybir.dt.float32
I32 = mybir.dt.int32
BF16 = mybir.dt.bfloat16
AF = mybir.ActivationFunctionType
ALU = mybir.AluOpType
AX = mybir.AxisListType.X

D, FF, TPC, TB = 1024, 4096, 2048, 1024
NC, NF = 8, 32
S64 = float(2.0 ** 64)
S64I = float(2.0 ** -64)
INV1024 = 1.0 / 1024.0
INV127 = 1.0 / 127.0
MAGIC = 12582912.0


class Trk:
    def __init__(self, ctx, nc, name):
        self.h = ctx.enter_context(nc.semaphore(name))
        self.v = 0

    def inc(self, inst, k=1):
        inst.then_inc(self.h, k)
        self.v += k
        return self.v


def build_program(stage="full"):
    nc = bass.Bass("TRN2", target_bir_lowering=False, detect_race_conditions=False)

    x_in = nc.dram_tensor("x_in", [TPC, D], F32, kind="ExternalInput").ap()
    w1_in = nc.dram_tensor("w1_in", [FF, D], F32, kind="ExternalInput").ap()
    b1_in = nc.dram_tensor("b1_in", [FF], F32, kind="ExternalInput").ap()
    w2_in = nc.dram_tensor("w2_in", [D, FF], F32, kind="ExternalInput").ap()
    b2_in = nc.dram_tensor("b2_in", [D], F32, kind="ExternalInput").ap()
    wx_in = nc.dram_tensor("wx_in", [2 * D, D], F32, kind="ExternalInput").ap()
    bx_in = nc.dram_tensor("bx_in", [2 * D], F32, kind="ExternalInput").ap()
    wdt_in = nc.dram_tensor("wdt_in", [D, D], F32, kind="ExternalInput").ap()
    bdt_in = nc.dram_tensor("bdt_in", [D], F32, kind="ExternalInput").ap()
    alog_in = nc.dram_tensor("alog_in", [D], F32, kind="ExternalInput").ap()
    gsl_in = nc.dram_tensor("gsl_in", [11 * 128, D], F32, kind="ExternalInput").ap()
    ident_in = nc.dram_tensor("ident_in", [128, 128], F32, kind="ExternalInput").ap()
    mu_in = nc.dram_tensor("mu_in", [128, 1], I32, kind="ExternalInput").ap()
    sbit_in = nc.dram_tensor("sbit_in", [128, 48], I32, kind="ExternalInput").ap()
    y_out = nc.dram_tensor("y_out", [TPC, D], F32, kind="ExternalOutput").ap()

    wq1_s = nc.dram_tensor("wq1_s", [FF, D], BF16, kind="Internal").ap()
    wq2_s = nc.dram_tensor("wq2_s", [D, FF], BF16, kind="Internal").ap()
    wq1T_s = nc.dram_tensor("wq1T_s", [D, FF], BF16, kind="Internal").ap()
    wq2T_s = nc.dram_tensor("wq2T_s", [FF, D], BF16, kind="Internal").ap()
    wx_s = nc.dram_tensor("wx_s", [2 * D, D], BF16, kind="Internal").ap()
    wdt_s = nc.dram_tensor("wdt_s", [D, D], BF16, kind="Internal").ap()
    u_sp = nc.dram_tensor("u_sp", [16 * 128, TB], F32, kind="Internal").ap()
    p_sp = nc.dram_tensor("p_sp", [16 * 128, TB], F32, kind="Internal").ap()
    c_sp = nc.dram_tensor("c_sp", [16 * 128, TB], F32, kind="Internal").ap()
    bounce = nc.dram_tensor("bounce", [8, 128], F32, kind="Internal").ap()
    cc_in = nc.dram_tensor("cc_in", [128, 16], F32, kind="Internal")
    cc_out = nc.dram_tensor("cc_out", [8 * 128, 16], F32, kind="Internal")
    cc2_in = nc.dram_tensor("cc2_in", [128, 4], F32, kind="Internal")
    cc2_out = nc.dram_tensor("cc2_out", [8 * 128, 4], F32, kind="Internal")

    ctx = ExitStack()
    SBUF = lambda name, shape, dt, **kw: ctx.enter_context(nc.sbuf_tensor(name, shape, dt, **kw))

    resid = {}
    for c in range(NC):
        for tb in range(2):
            resid[(c, tb)] = SBUF(f"res_{c}_{tb}", [128, TB], F32)
    spare = [SBUF("spare", [128, TB], F32)]
    ident = SBUF("ident", [128, 128], F32)
    ones_r = SBUF("ones_r", [1, 128], F32)
    ones_cb = SBUF("ones_cb", [128, 1], BF16)
    ones_cf = SBUF("ones_cf", [128, 1], F32)
    b1sb = SBUF("b1sb", [128, NF], F32)
    hb2sb = SBUF("hb2sb", [128, NC], F32)
    hb2row = SBUF("hb2row", [1, D], F32)
    bxsb = SBUF("bxsb", [128, 16], F32)
    bdtsb = SBUF("bdtsb", [128, NC], F32)
    asb = SBUF("asb", [128, NC], F32)
    musb = SBUF("musb", [128, 1], I32)
    sbits = SBUF("sbits", [128, 48], I32)
    zcol = SBUF("zcol", [128, 1], F32)
    bigcol = SBUF("bigcol", [128, 1], F32)
    zeros_h = SBUF("zeros_h", [128, 512], F32)
    ends = SBUF("ends", [128, 16], F32)
    chainP = SBUF("chainP", [128, NC], F32)
    chainS = SBUF("chainS", [128, NC], F32)
    chainS2 = SBUF("chainS2", [128, NC], F32)
    r1s_t = SBUF("r1s_t", [128, NC], F32)
    q2_t = SBUF("q2_t", [128, NC], F32)
    wsrow = SBUF("wsrow", [1, 16], F32)
    rbc = SBUF("rbc", [128, 4], F32)
    gacp = SBUF("gacp", [128, 16], F32)
    gath2 = SBUF("gath2", [128, 32], F32)

    pA = ctx.enter_context(nc.psum_tensor("pA", [128, 1024], F32))
    pB = ctx.enter_context(nc.psum_tensor("pB", [128, 1024], F32))
    pC = ctx.enter_context(nc.psum_tensor("pC", [128, 1024], F32))
    pD = ctx.enter_context(nc.psum_tensor("pD", [128, 1024], F32))
    PS4 = [pA, pB, pC, pD]

    def _fence(eng, F):
        eng.wait_ge(pe.h, F[0])
        eng.wait_ge(dv.h, F[1])
        eng.wait_ge(ac.h, F[2])

    sy1 = Trk(ctx, nc, "sy1")
    sy2 = Trk(ctx, nc, "sy2")
    sy3 = Trk(ctx, nc, "sy3")
    sy4 = Trk(ctx, nc, "sy4")
    sy5 = Trk(ctx, nc, "sy5")
    sc1 = Trk(ctx, nc, "sc1")
    sc2 = Trk(ctx, nc, "sc2")
    gp1 = Trk(ctx, nc, "gp1")
    gp2 = Trk(ctx, nc, "gp2")
    gp3 = Trk(ctx, nc, "gp3")
    gp4 = Trk(ctx, nc, "gp4")
    sy6 = Trk(ctx, nc, "sy6")
    pe = Trk(ctx, nc, "pe")
    dv = Trk(ctx, nc, "dv")
    ac = Trk(ctx, nc, "ac")
    st = Trk(ctx, nc, "st")
    ccs = Trk(ctx, nc, "ccs")

    with ExitStack() as b0:
        xtm = b0.enter_context(nc.sbuf_tensor("xtm", [128, 4 * D], F32))
        gb1 = b0.enter_context(nc.sbuf_tensor("gb1", [128, 2 * D], F32))
        gca = b0.enter_context(nc.sbuf_tensor("gca", [128, 32], F32))
        atmp = b0.enter_context(nc.sbuf_tensor("atmp", [128, 2 * NC], F32))
        blk = b0.enter_context(nc.Block())
        F_ = (pe.v, dv.v, ac.v)

        pe0, dv0, ac0 = pe.v, dv.v, ac.v
        gp1_0, gp2_0 = gp1.v, gp2.v
        xs_sem = {0: sy1, 2: sy2, 1: sc1, 3: sc2}
        ga_sem = [sy3, sy4]
        sy_x = [None] * 16
        sy_g1 = [None] * 11
        for i in range(16):
            sy_x[i] = (xs_sem[i % 4], xs_sem[i % 4].v + (i // 4 + 1) * 16)
        for i in range(11):
            sy_g1[i] = (ga_sem[i % 2], ga_sem[i % 2].v + (i // 2 + 1) * 16)
        sy5_setup_target = sy5.v + 76 * 16
        gac = gacp
        ms_gam = [0]

        @blk.sync
        def _(sync):
            _fence(sync, F_)
            sy5.inc(sync.dma_start(out=ident[:], in_=ident_in), 16)
            sy5.inc(sync.dma_start(out=musb[:], in_=mu_in), 16)
            sy5.inc(sync.dma_start(out=sbits[:], in_=sbit_in), 16)
            for c in range(NF):
                sy5.inc(sync.dma_start(out=b1sb[:, c:c + 1], in_=b1_in[c * 128:(c + 1) * 128, None]), 16)
            for c in range(NC):
                sy5.inc(sync.dma_start(out=hb2sb[:, c:c + 1], in_=b2_in[c * 128:(c + 1) * 128, None]), 16)
            sy5.inc(sync.dma_start(out=hb2row[:], in_=b2_in[None, :]), 16)
            for c in range(16):
                sy5.inc(sync.dma_start(out=bxsb[:, c:c + 1], in_=bx_in[c * 128:(c + 1) * 128, None]), 16)
            for c in range(NC):
                sy5.inc(sync.dma_start(out=bdtsb[:, c:c + 1], in_=bdt_in[c * 128:(c + 1) * 128, None]), 16)
            for c in range(NC):
                sy5.inc(sync.dma_start(out=atmp[:, c:c + 1], in_=alog_in[c * 128:(c + 1) * 128, None]), 16)
            assert sy5.v == sy5_setup_target
            for tt in range(0, 16, 2):
                if tt >= 4:
                    sync.wait_ge(pe.h, pe0 + tt - 3)
                v = xs_sem[tt % 4].inc(sync.dma_start(
                    out=xtm[:, (tt % 4) * D:(tt % 4 + 1) * D],
                    in_=x_in[tt * 128:(tt + 1) * 128, :]), 16)
                assert (xs_sem[tt % 4], v) == sy_x[tt]
            for i in range(11):
                if i >= 2:
                    sync.wait_ge(dv.h, dv0 + 16 + (i - 1))
                v = ga_sem[i % 2].inc(sync.dma_start(
                    out=gb1[:, (i % 2) * D:(i % 2 + 1) * D],
                    in_=gsl_in[i * 128:(i + 1) * 128, :]), 16)
                assert (ga_sem[i % 2], v) == sy_g1[i]
            for sem in (sy1, sy2, sy3, sy4, sy5):
                sync.wait_ge(sem.h, sem.v)

        @blk.tensor
        def _(tensor):
            _fence(tensor, F_)
            tensor.wait_ge(sy5.h, sy5_setup_target)
            for tt in range(16):
                tensor.wait_ge(sy_x[tt][0].h, sy_x[tt][1])
                if tt >= 4:
                    tensor.wait_ge(dv.h, dv0 + tt - 3)
                ps = PS4[tt % 4]
                inst = None
                for c in range(NC):
                    inst = tensor.matmul(ps[:, c * 128:(c + 1) * 128],
                                         xtm[:, (tt % 4) * D + c * 128:(tt % 4) * D + (c + 1) * 128],
                                         ident[:], start=(c % 4 == 0), stop=(c % 4 == 3))
                pe.inc(inst, 1)

        @blk.vector
        def _(vector):
            _fence(vector, F_)
            vector.memset(ones_r[:], 1.0)
            vector.memset(ones_cb[:], 1.0)
            vector.memset(ones_cf[:], 1.0)
            vector.memset(zcol[:], 0.0)
            vector.memset(bigcol[:], S64)
            vector.memset(zeros_h[:], 0.0)
            vector.memset(gac[:], 0.0)
            vector.memset(chainP[:, 0:NC], S64)
            vector.memset(chainS[:], 0.0)
            for tt in range(16):
                vector.wait_ge(pe.h, pe0 + tt + 1)
                ps = PS4[tt % 4]
                tb, tl = tt // 8, tt % 8
                inst = None
                for c in range(NC):
                    inst = vector.tensor_copy(resid[(c, tb)][:, tl * 128:(tl + 1) * 128],
                                              ps[:, c * 128:(c + 1) * 128])
                dv.inc(inst, 1)
            assert dv.v == dv0 + 16
            for i in range(11):
                vector.wait_ge(sy_g1[i][0].h, sy_g1[i][1])
                dv.inc(vector.tensor_reduce(gca[:, i:i + 1], gb1[:, (i % 2) * D:(i % 2 + 1) * D],
                                            AX, ALU.add, apply_absolute_value=True), 1)
            vector.wait_ge(dv.h, dv.v)
            vector.tensor_reduce(gac[:, 0:1], gca[:, 0:4], AX, ALU.add)
            vector.tensor_reduce(gac[:, 1:2], gca[:, 4:8], AX, ALU.add)
            vector.tensor_reduce(gac[:, 2:3], gca[:, 8:10], AX, ALU.add)
            ms_gam[0] = dv.inc(vector.tensor_copy(gac[:, 3:4], gca[:, 10:11]), 1)
            vector.wait_ge(ac.h, ac0 + 3)
            v = dv.inc(vector.tensor_scalar(asb[:], atmp[:, 0:NC], 0.0, None, ALU.max), 1)
            vector.wait_ge(dv.h, v)
            v = dv.inc(vector.scalar_tensor_tensor(atmp[:, 0:NC], asb[:], -1.0, atmp[:, NC:2 * NC],
                                                   ALU.mult, ALU.subtract), 1)
            vector.wait_ge(dv.h, v)
            dv.inc(vector.tensor_copy(asb[:], atmp[:, 0:NC]), 1)

        @blk.scalar
        def _(scalar):
            _fence(scalar, F_)
            for tt in range(1, 16, 2):
                if tt >= 4:
                    scalar.wait_ge(pe.h, pe0 + tt - 3)
                v = xs_sem[tt % 4].inc(scalar.dma_start(
                    out=xtm[:, (tt % 4) * D:(tt % 4 + 1) * D],
                    in_=x_in[tt * 128:(tt + 1) * 128, :]), 16)
                assert (xs_sem[tt % 4], v) == sy_x[tt]
            for sem in (sc1, sc2):
                scalar.wait_ge(sem.h, sem.v)
            scalar.wait_ge(sy5.h, sy5_setup_target)
            v = ac.inc(scalar.activation(atmp[:, NC:2 * NC], atmp[:, 0:NC], AF.Abs), 1)
            scalar.wait_ge(ac.h, v)
            v = ac.inc(scalar.activation(asb[:], atmp[:, NC:2 * NC], AF.Exp, scale=-1.0), 1)
            scalar.wait_ge(ac.h, v)
            ac.inc(scalar.activation(atmp[:, NC:2 * NC], asb[:], AF.Ln, bias=1.0), 1)

        @blk.gpsimd
        def _(gpsimd):
            _fence(gpsimd, F_)
            gpsimd.wait_ge(dv.h, ms_gam[0])
            v = gp1.inc(gpsimd.dma_start(out=cc2_in.ap(), in_=gac[:, 0:4]), 16)
            gpsimd.wait_ge(gp1.h, v)
            v = ccs.inc(gpsimd.collective_compute(
                "AllGather", ALU.bypass, replica_groups=[list(range(8))],
                ins=[cc2_in.ap().opt()], outs=[cc2_out.ap().opt()]), 1)
            gpsimd.wait_ge(ccs.h, v)
            for r in range(8):
                gp2.inc(gpsimd.dma_start(out=gath2[:, r * 4:(r + 1) * 4],
                                         in_=cc2_out.ap()[r * 128:(r + 1) * 128, :]), 16)
            gpsimd.wait_ge(gp2.h, gp2.v)



    with nc.Block() as blk:
        F_ = (pe.v, dv.v, ac.v)
        pe0, dv0 = pe.v, dv.v
        nelem = {0: FF * D, 1: D * FF, 2: 2 * D * D, 3: D * D}
        ms = {}

        @blk.vector
        def _(vector):
            _fence(vector, F_)
            vector.wait_ge(pe.h, pe0 + 1)
            v = dv.inc(vector.tensor_copy(wsrow[0:1, 0:4], pA[0:1, 0:4]), 1)
            vector.wait_ge(dv.h, v)
            for wi in range(4):
                v = dv.inc(vector.tensor_scalar(wsrow[0:1, 4 + wi:5 + wi], wsrow[0:1, wi:wi + 1],
                                                1.0 / nelem[wi], 1e-5, ALU.mult, ALU.add), 1)
            vector.wait_ge(dv.h, v)
            ms["recip"] = dv.inc(vector.reciprocal(wsrow[0:1, 8:12], wsrow[0:1, 4:8]), 1)
            vector.wait_ge(pe.h, pe0 + 2)
            dv.inc(vector.tensor_copy(rbc[:], pB[:, 0:4]), 1)
            vector.tensor_scalar(hb2sb[:], hb2sb[:], 0.5, None, ALU.mult)
            dv.inc(vector.tensor_scalar(hb2row[:], hb2row[:], 0.5, None, ALU.mult), 1)

        @blk.tensor
        def _(tensor):
            _fence(tensor, F_)
            inst = None
            for r in range(8):
                inst = tensor.matmul(pA[0:1, 0:4], ones_cf[:], gath2[:, r * 4:(r + 1) * 4],
                                     start=(r == 0), stop=(r == 7))
            pe.inc(inst, 1)
            tensor.wait_ge(dv.h, ms["recip"])
            pe.inc(tensor.matmul(pB[:, 0:4], ones_r[:], wsrow[0:1, 8:12], start=True, stop=True), 1)

    with ExitStack() as b0c:
        srcA = b0c.enter_context(nc.sbuf_tensor("srcA", [128, 4 * D], F32))
        qiA = b0c.enter_context(nc.sbuf_tensor("qiA", [128, D], F32))
        qbA = b0c.enter_context(nc.sbuf_tensor("qbA", [128, 4 * D], BF16))
        srcB = b0c.enter_context(nc.sbuf_tensor("srcB", [128, 4 * D], F32))
        qiB = b0c.enter_context(nc.sbuf_tensor("qiB", [128, D], F32))
        qfB = b0c.enter_context(nc.sbuf_tensor("qfB", [128, D], F32))
        qbB = b0c.enter_context(nc.sbuf_tensor("qbB", [128, 4 * D], BF16))
        rb2 = b0c.enter_context(nc.sbuf_tensor("rb2", [128, 4 * 1024], BF16))
        blk = b0c.enter_context(nc.Block())
        F_ = (pe.v, dv.v, ac.v)

        dv0, sc1_0 = dv.v, sc1.v
        la_sem = [sy1, sy2]
        sy_loadA = [(la_sem[i % 2], la_sem[i % 2].v + (i // 2 + 1) * 16) for i in range(32)]
        sc_loadB2 = [(sy1, sy1.v + 16 * 16 + (i // 2 + 1) * 16) if i % 2 == 0
                     else (sc2, sc2.v + (i // 2 + 1) * 16) for i in range(64)]
        st_a0 = sy3.v
        tilesB = []
        for r0 in range(0, D, 128):
            for c0 in range(0, FF, 1024):
                tilesB.append((1, w2_in[r0:r0 + 128, c0:c0 + 1024], wq2_s[r0:r0 + 128, c0:c0 + 1024], True))
        for r0 in range(0, 2 * D, 128):
            tilesB.append((2, wx_in[r0:r0 + 128, :], wx_s[r0:r0 + 128, :], False))
        for r0 in range(0, D, 128):
            tilesB.append((3, wdt_in[r0:r0 + 128, :], wdt_s[r0:r0 + 128, :], False))
        nB = len(tilesB)
        order = [("A", i) for i in range(32)] + [("B", j) for j in range(nB)]
        msA, msB = [0] * 32, [0] * nB
        for k, (s, i) in enumerate(order):
            (msA if s == "A" else msB)[i] = dv0 + k + 1
        sy4_0, gp2_0b = sy4.v, gp2.v

        @blk.sync
        def _(sync):
            _fence(sync, F_)
            for i in range(32):
                if i >= 4:
                    sync.wait_ge(dv.h, msA[i - 4])
                v = la_sem[i % 2].inc(sync.dma_start(
                    out=srcA[:, (i % 4) * D:(i % 4 + 1) * D],
                    in_=w1_in[i * 128:(i + 1) * 128, :]), 16)
                assert (la_sem[i % 2], v) == sy_loadA[i]
                if i >= 4:
                    sy3.inc(sync.dma_start(out=wq1_s[(i - 4) * 128:(i - 3) * 128, :],
                                           in_=qbA[:, ((i - 4) % 4) * D:((i - 4) % 4 + 1) * D]), 16)
            for j in range(28, 32):
                sync.wait_ge(dv.h, msA[j])
                sy3.inc(sync.dma_start(out=wq1_s[j * 128:(j + 1) * 128, :],
                                       in_=qbA[:, (j % 4) * D:(j % 4 + 1) * D]), 16)
            for i in range(0, len(tilesB), 2):
                if i >= 4:
                    sync.wait_ge(dv.h, msB[i - 4])
                v = sy1.inc(sync.dma_start(
                    out=srcB[:, (i % 4) * D:(i % 4 + 1) * D], in_=tilesB[i][1]), 16)
                assert (sy1, v) == sc_loadB2[i]
            for sem in (sy1, sy2, sy3):
                sync.wait_ge(sem.h, sem.v)

        @blk.vector
        def _(vector):
            _fence(vector, F_)
            for s, i in order:
                if s == "A":
                    vector.wait_ge(sy_loadA[i][0].h, sy_loadA[i][1])
                    if i >= 4:
                        vector.wait_ge(sy3.h, st_a0 + (i - 3) * 16)
                    src = srcA[:, (i % 4) * D:(i % 4 + 1) * D]
                    vector.tensor_scalar(qiA[:], src, rbc[:, 0:1], None, ALU.mult)
                    vector.tensor_scalar(qiA[:], qiA[:], MAGIC, MAGIC, ALU.add, ALU.subtract)
                    v = dv.inc(vector.tensor_scalar(qbA[:, (i % 4) * D:(i % 4 + 1) * D], qiA[:],
                                                    1.0, -1.0, ALU.min, ALU.max), 1)
                    assert v == msA[i]
                else:
                    wi, _, dst, halve = tilesB[i]
                    vector.wait_ge(sc_loadB2[i][0].h, sc_loadB2[i][1])
                    if i >= 4:
                        vector.wait_ge(sc1.h, sc1_0 + (i - 3) * 16)
                    src = srcB[:, (i % 4) * D:(i % 4 + 1) * D]
                    vector.tensor_scalar(qiB[:], src, rbc[:, wi:wi + 1], None, ALU.mult)
                    vector.tensor_scalar(qiB[:], qiB[:], MAGIC, MAGIC, ALU.add, ALU.subtract)
                    if halve:
                        vector.tensor_scalar(qfB[:], qiB[:], 1.0, -1.0, ALU.min, ALU.max)
                        v = dv.inc(vector.tensor_scalar(qbB[:, (i % 4) * D:(i % 4 + 1) * D], qfB[:],
                                                        0.5, None, ALU.mult), 1)
                    else:
                        v = dv.inc(vector.tensor_scalar(qbB[:, (i % 4) * D:(i % 4 + 1) * D], qiB[:],
                                                        1.0, -1.0, ALU.min, ALU.max), 1)
                    assert v == msB[i]

        @blk.scalar
        def _(scalar):
            _fence(scalar, F_)
            def _loadB_odd(j):
                if j >= 4:
                    scalar.wait_ge(dv.h, msB[j - 4])
                v = sc2.inc(scalar.dma_start(
                    out=srcB[:, (j % 4) * D:(j % 4 + 1) * D], in_=tilesB[j][1]), 16)
                assert (sc2, v) == sc_loadB2[j]
            _loadB_odd(1)
            _loadB_odd(3)
            for i, (wi, _, dst, halve) in enumerate(tilesB):
                if i % 2 == 1 and i + 4 < len(tilesB):
                    _loadB_odd(i + 4)
                scalar.wait_ge(dv.h, msB[i])
                sc1.inc(scalar.dma_start(out=dst, in_=qbB[:, (i % 4) * D:(i % 4 + 1) * D]), 16)
                if 28 <= i <= 31:
                    scalar.wait_ge(sc1.h, sc1_0 + (i + 1) * 16)
                    for f in range((i - 28) * 8, (i - 28) * 8 + 8):
                        if f >= 4:
                            scalar.wait_ge(gp2.h, gp2_0b + (f - 3) * 16)
                        sy4.inc(scalar.dma_start(
                            out=rb2[:, (f % 4) * 1024:(f % 4 + 1) * 1024],
                            in_=wq2_s[0:1024, f * 128:(f + 1) * 128], transpose=True), 16)
            scalar.wait_ge(sc1.h, sc1.v)
            scalar.wait_ge(sc2.h, sc2.v)
            scalar.wait_ge(sy4.h, sy4.v)

        @blk.gpsimd
        def _(gpsimd):
            _fence(gpsimd, F_)
            for f in range(32):
                gpsimd.wait_ge(sy4.h, sy4_0 + (f + 1) * 16)
                gp2.inc(gpsimd.dma_start(out=wq2T_s[f * 128:(f + 1) * 128, :],
                                         in_=rb2[:, (f % 4) * 1024:(f % 4 + 1) * 1024]), 16)
            gpsimd.wait_ge(gp2.h, gp2.v)


    def emit_ln(tb, xln, tag):
        with ExitStack() as sscope:
            sq = sscope.enter_context(nc.sbuf_tensor(f"lnsq{tag}", [128, 2 * TB], BF16))
            rowb = sscope.enter_context(nc.sbuf_tensor(f"lnrow{tag}", [1, 4 * TB], F32))
            tmpt = sscope.enter_context(nc.sbuf_tensor(f"lntmp{tag}", [128, 2 * TB], F32))
            blk = sscope.enter_context(nc.Block())
            F_ = (pe.v, dv.v, ac.v)
            pe0, dv0, ac0 = pe.v, dv.v, ac.v
            R = lambda i, o=0, w=TB: rowb[0:1, i * TB + o:i * TB + o + w]
            ms = {}

            @blk.vector
            def _(vector):
                _fence(vector, F_)
                vector.wait_ge(pe.h, pe0 + 8)
                v = dv.inc(vector.tensor_copy(R(0), pC[0:1, 0:TB]), 1)
                v = dv.inc(vector.tensor_copy(R(1), pD[0:1, 0:TB]), 1)
                vector.wait_ge(dv.h, v)
                v = dv.inc(vector.tensor_scalar(R(2), R(0), INV1024, None, ALU.mult), 1)
                v = dv.inc(vector.tensor_scalar(R(3), R(1), INV1024, None, ALU.mult), 1)
                vector.wait_ge(dv.h, v)
                v = dv.inc(vector.tensor_tensor(R(1), R(2), R(2), ALU.mult), 1)
                vector.wait_ge(dv.h, v)
                v = dv.inc(vector.tensor_tensor(R(0), R(3), R(1), ALU.subtract), 1)
                vector.wait_ge(dv.h, v)
                ms["vareps"] = dv.inc(vector.tensor_scalar(R(1), R(0), 1e-5, None, ALU.add), 1)
                vector.wait_ge(ac.h, ac0 + 10)
                ms["brow"] = dv.inc(vector.scalar_tensor_tensor(R(3), R(2), -1.0, R(0),
                                                               ALU.mult, ALU.mult), 1)
                vector.wait_ge(pe.h, pe0 + 9)
                T = lambda c: tmpt[:, (c % 2) * TB:(c % 2 + 1) * TB]
                dv.inc(vector.tensor_tensor(T(0), resid[(0, tb)][:], pC[:, 0:TB], ALU.mult), 1)
                for c in range(1, NC):
                    dv.inc(vector.tensor_tensor(T(c), resid[(c, tb)][:], pC[:, 0:TB], ALU.mult), 1)
                    dv.inc(vector.tensor_tensor(xln[c - 1][:], T(c - 1), pD[:, 0:TB], ALU.add), 1)
                dv.inc(vector.tensor_tensor(xln[NC - 1][:], T(NC - 1), pD[:, 0:TB], ALU.add), 1)

            @blk.scalar
            def _(scalar):
                _fence(scalar, F_)
                for c in range(NC):
                    if c >= 2:
                        scalar.wait_ge(pe.h, pe0 + c - 1)
                    ac.inc(scalar.activation(sq[:, (c % 2) * TB:(c % 2 + 1) * TB],
                                             resid[(c, tb)][:], AF.Square), 1)
                scalar.wait_ge(dv.h, ms["vareps"])
                v = ac.inc(scalar.activation(R(3), R(1), AF.Ln), 1)
                scalar.wait_ge(ac.h, v)
                v = ac.inc(scalar.activation(R(0), R(3), AF.Exp, scale=-0.5), 1)
                assert v == ac0 + 10

            @blk.tensor
            def _(tensor):
                _fence(tensor, F_)
                for c in range(NC):
                    tensor.wait_ge(ac.h, ac0 + c + 1)
                    rs_ = resid[(c, tb)]
                    ss = sq[:, (c % 2) * TB:(c % 2 + 1) * TB]
                    tensor.matmul(pC[0:1, 0:512], ones_cf[:], rs_[:, 0:512], start=(c == 0), stop=(c == 7))
                    tensor.matmul(pC[0:1, 512:1024], ones_cf[:], rs_[:, 512:1024], start=(c == 0), stop=(c == 7))
                    tensor.matmul(pD[0:1, 0:512], ones_cb[:], ss[:, 0:512], start=(c == 0), stop=(c == 7))
                    inst = tensor.matmul(pD[0:1, 512:1024], ones_cb[:], ss[:, 512:1024], start=(c == 0), stop=(c == 7))
                    pe.inc(inst, 1)
                tensor.wait_ge(ac.h, ac0 + 10)
                tensor.wait_ge(dv.h, ms["brow"])
                tensor.matmul(pC[:, 0:512], ones_r[:], R(0, 0, 512), start=True, stop=True)
                tensor.matmul(pC[:, 512:1024], ones_r[:], R(0, 512, 512), start=True, stop=True)
                tensor.matmul(pD[:, 0:512], ones_r[:], R(3, 0, 512), start=True, stop=True)
                inst = tensor.matmul(pD[:, 512:1024], ones_r[:], R(3, 512, 512), start=True, stop=True)
                pe.inc(inst, 1)

    def emit_mm1(tb, xln, H, tag):
        with ExitStack() as s:
            wslot = s.enter_context(nc.sbuf_tensor(f"w1s{tag}", [128, 2 * 4096], BF16))
            blk = s.enter_context(nc.Block())
            F_ = (pe.v, dv.v, ac.v)
            pe0, ac0 = pe.v, ac.v
            fg_sem = [sy1, sy2]
            sy_fg = [(fg_sem[fg % 2], fg_sem[fg % 2].v + (fg // 2 + 1) * 128) for fg in range(8)]

            @blk.sync
            def _(sync):
                _fence(sync, F_)
                for fg in range(8):
                    if fg >= 2:
                        sync.wait_ge(pe.h, pe0 + (fg - 1) * 4)
                    v = None
                    for k in range(8):
                        v = fg_sem[fg % 2].inc(sync.dma_start(
                            out=wslot[:, (fg % 2) * 4096 + k * 512:(fg % 2) * 4096 + (k + 1) * 512],
                            in_=wq1_s[fg * 512:(fg + 1) * 512, k * 128:(k + 1) * 128],
                            transpose=True), 16)
                    assert (fg_sem[fg % 2], v) == sy_fg[fg]
                for sem in (sy1, sy2):
                    sync.wait_ge(sem.h, sem.v)

            @blk.tensor
            def _(tensor):
                _fence(tensor, F_)
                for fg in range(8):
                    tensor.wait_ge(sy_fg[fg][0].h, sy_fg[fg][1])
                    for fcl in range(4):
                        fc = fg * 4 + fcl
                        ps = pA if fc % 2 == 0 else pB
                        if fc >= 2:
                            tensor.wait_ge(ac.h, ac0 + fc - 1)
                        inst = None
                        for k in range(8):
                            base = (fg % 2) * 4096 + k * 512 + fcl * 128
                            tensor.matmul(ps[:, 0:512], wslot[:, base:base + 128],
                                          xln[k][:, 0:512], start=(k == 0), stop=(k == 7))
                            inst = tensor.matmul(ps[:, 512:1024], wslot[:, base:base + 128],
                                                 xln[k][:, 512:1024], start=(k == 0), stop=(k == 7))
                        pe.inc(inst, 1)

            @blk.scalar
            def _(scalar):
                _fence(scalar, F_)
                for fc in range(32):
                    scalar.wait_ge(pe.h, pe0 + fc + 1)
                    ps = pA if fc % 2 == 0 else pB
                    ac.inc(scalar.activation(H[fc][:], ps[:], AF.Relu,
                                             bias=b1sb[:, fc:fc + 1]), 1)

    def emit_mm2a(tb, H, tag):
        with ExitStack() as s:
            wslot = s.enter_context(nc.sbuf_tensor(f"w2s{tag}", [128, 4 * 1024], BF16))
            blk = s.enter_context(nc.Block())
            F_ = (pe.v, dv.v, ac.v)
            pe0, dv0 = pe.v, dv.v
            k_sem = {0: sy1, 2: sy2, 1: sc1, 3: sc2}
            sy_k = {}
            cnt = {0: sy1.v, 2: sy2.v, 1: sc1.v, 3: sc2.v}
            for sub in range(2):
                for k in range(32):
                    i = sub * 32 + k
                    cnt[i % 4] += 16
                    sy_k[(sub, k)] = (k_sem[i % 4], cnt[i % 4])

            def _loads(eng, par):
                for sub in range(2):
                    for k in range(32):
                        i = sub * 32 + k
                        if par is not None and i % 2 != par:
                            continue
                        if i >= 4:
                            eng.wait_ge(pe.h, pe0 + i - 3)
                        v = k_sem[i % 4].inc(eng.dma_start(
                            out=wslot[:, (i % 4) * 1024:(i % 4 + 1) * 1024],
                            in_=wq2T_s[k * 128:(k + 1) * 128, 0:1024]), 16)
                        assert (k_sem[i % 4], v) == sy_k[(sub, k)]

            @blk.sync
            def _(sync):
                _fence(sync, F_)
                _loads(sync, 0)
                for sem in (sy1, sy2):
                    sync.wait_ge(sem.h, sem.v)

            @blk.scalar
            def _(scalar):
                _fence(scalar, F_)
                _loads(scalar, 1)
                for sem in (sc1, sc2):
                    scalar.wait_ge(sem.h, sem.v)

            @blk.tensor
            def _(tensor):
                _fence(tensor, F_)
                for sub in range(2):
                    if sub == 1:
                        tensor.wait_ge(dv.h, dv0 + 1)
                    for k in range(32):
                        i = sub * 32 + k
                        tensor.wait_ge(sy_k[(sub, k)][0].h, sy_k[(sub, k)][1])
                        inst = None
                        for dout in range(8):
                            ps = PS4[dout // 2][:, (dout % 2) * 512:(dout % 2 + 1) * 512]
                            inst = tensor.matmul(
                                ps, wslot[:, (i % 4) * 1024 + dout * 128:(i % 4) * 1024 + (dout + 1) * 128],
                                H[k][:, sub * 512:(sub + 1) * 512],
                                start=(k == 0), stop=(k == 31))
                        pe.inc(inst, 1)

            @blk.vector
            def _(vector):
                _fence(vector, F_)
                for sub in range(2):
                    vector.wait_ge(pe.h, pe0 + (sub + 1) * 32)
                    inst = None
                    for dout in range(8):
                        ps = PS4[dout // 2][:, (dout % 2) * 512:(dout % 2 + 1) * 512]
                        rs_ = resid[(dout, tb)][:, sub * 512:(sub + 1) * 512]
                        inst = vector.scalar_tensor_tensor(rs_, ps, hb2sb[:, dout:dout + 1],
                                                           rs_, ALU.add, ALU.add)
                    dv.inc(inst, 1)

    def emit_mm2b(tb, H, tag):
        with ExitStack() as s:
            wslot = s.enter_context(nc.sbuf_tensor(f"w2t{tag}", [128, 4 * 1024], BF16))
            otile = s.enter_context(nc.sbuf_tensor(f"ot{tag}", [128, 4 * 1024], F32))
            blk = s.enter_context(nc.Block())
            F_ = (pe.v, dv.v, ac.v)
            pe0, dv0, st0 = pe.v, dv.v, st.v
            pe_after = lambda tq, k: pe0 + tq * 33 + 1 + (k + 1)
            k_sem = {0: sy1, 2: sy2, 1: sc1, 3: sc2}
            sy_k = {}
            cnt = {0: sy1.v, 2: sy2.v, 1: sc1.v, 3: sc2.v}
            for tq in range(2):
                for k in range(32):
                    i = tq * 32 + k
                    cnt[i % 4] += 16
                    sy_k[(tq, k)] = (k_sem[i % 4], cnt[i % 4])
            st_o = [st0 + (j + 1) * 16 for j in range(8)]
            msE = [dv0 + j + 1 for j in range(8)]

            def _loads(eng, par):
                for tq in range(2):
                    for k in range(32):
                        i = tq * 32 + k
                        if par is not None and i % 2 != par:
                            continue
                        if i >= 4:
                            i2 = i - 4
                            eng.wait_ge(pe.h, pe_after(i2 // 32, i2 % 32))
                        v = k_sem[i % 4].inc(eng.dma_start(
                            out=wslot[:, (i % 4) * 1024:(i % 4 + 1) * 1024],
                            in_=wq2T_s[k * 128:(k + 1) * 128, 0:1024]), 16)
                        assert (k_sem[i % 4], v) == sy_k[(tq, k)]

            @blk.sync
            def _(sync):
                _fence(sync, F_)
                _loads(sync, 0)
                for sem in (sy1, sy2):
                    sync.wait_ge(sem.h, sem.v)

            @blk.scalar
            def _(scalar):
                _fence(scalar, F_)
                _loads(scalar, 1)
                for sem in (sc1, sc2):
                    scalar.wait_ge(sem.h, sem.v)

            @blk.tensor
            def _(tensor):
                _fence(tensor, F_)
                for tq in range(2):
                    if tq == 1:
                        tensor.wait_ge(dv.h, msE[3])
                    inst = None
                    for tcl in range(4):
                        tc = tq * 4 + tcl
                        ps = PS4[tcl]
                        for c in range(NC):
                            inst = tensor.matmul(ps[:, c * 128:(c + 1) * 128],
                                                 resid[(c, tb)][:, tc * 128:(tc + 1) * 128],
                                                 ident[:], start=(c in (0, 4)), stop=False)
                        tensor.matmul(ps[:, 0:512], ones_r[:], hb2row[0:1, 0:512],
                                      start=False, stop=False)
                        inst = tensor.matmul(ps[:, 512:1024], ones_r[:], hb2row[0:1, 512:1024],
                                             start=False, stop=False)
                    pe.inc(inst, 1)
                    for k in range(32):
                        tensor.wait_ge(sy_k[(tq, k)][0].h, sy_k[(tq, k)][1])
                        i = tq * 32 + k
                        inst = None
                        for tcl in range(4):
                            tc = tq * 4 + tcl
                            ps = PS4[tcl]
                            tensor.matmul(ps[:, 0:512], H[k][:, tc * 128:(tc + 1) * 128],
                                          wslot[:, (i % 4) * 1024:(i % 4) * 1024 + 512],
                                          start=False, stop=(k == 31))
                            inst = tensor.matmul(ps[:, 512:1024], H[k][:, tc * 128:(tc + 1) * 128],
                                                 wslot[:, (i % 4) * 1024 + 512:(i % 4 + 1) * 1024],
                                                 start=False, stop=(k == 31))
                        pe.inc(inst, 1)

            @blk.vector
            def _(vector):
                _fence(vector, F_)
                for tq in range(2):
                    vector.wait_ge(pe.h, pe0 + (tq + 1) * 33)
                    for tcl in range(4):
                        j = tq * 4 + tcl
                        if j >= 4:
                            vector.wait_ge(st.h, st_o[j - 4])
                        v = dv.inc(vector.tensor_copy(otile[:, (j % 4) * 1024:(j % 4 + 1) * 1024],
                                                      PS4[tcl][:]), 1)
                        assert v == msE[j]

            @blk.gpsimd
            def _(gpsimd):
                _fence(gpsimd, F_)
                for j in range(8):
                    gpsimd.wait_ge(dv.h, msE[j])
                    v = st.inc(gpsimd.dma_start(
                        out=y_out[tb * TB + j * 128:tb * TB + (j + 1) * 128, :],
                        in_=otile[:, (j % 4) * 1024:(j % 4 + 1) * 1024]), 16)
                    assert v == st_o[j]
                gpsimd.wait_ge(st.h, st.v)

    def debug_dump_resid():
        with nc.Block() as blk:
            F_ = (pe.v, dv.v, ac.v)
            @blk.sync
            def _(sync):
                _fence(sync, F_)
                for c in range(NC):
                    for tb in range(2):
                        st.inc(sync.dma_start(
                            out=y_out[(tb * 8 + c) * 128:(tb * 8 + c + 1) * 128, :],
                            in_=resid[(c, tb)][:]), 16)
                sync.wait_ge(st.h, st.v)
        build_program.resid_names = {k: t.name for k, t in resid.items()}
        return nc, ctx

    if stage == "x0":
        return debug_dump_resid()

    if stage == "rows":
        with nc.Block() as blk:
            F_ = (pe.v, dv.v, ac.v)
            @blk.gpsimd
            def _(gpsimd):
                _fence(gpsimd, F_)
                v = st.inc(gpsimd.dma_start(out=y_out[0:1, 0:16], in_=wsrow[:]), 16)
                v = st.inc(gpsimd.dma_start(out=y_out[1:129, 0:4], in_=rbc[:]), 16)
                v = st.inc(gpsimd.dma_start(out=y_out[129:257, 0:16], in_=gacp[:]), 16)
                gpsimd.wait_ge(st.h, st.v)
        build_program.resid_names = {}
        return nc, ctx

    if stage in ("wq1", "wq2"):
        src_t = wq1_s if stage == "wq1" else wq2_s
        with ExitStack() as dbg:
            tb16 = dbg.enter_context(nc.sbuf_tensor("dbg16", [128, 1024], BF16))
            tf32 = dbg.enter_context(nc.sbuf_tensor("dbg32", [128, 1024], F32))
            blk = dbg.enter_context(nc.Block())
            F_ = (pe.v, dv.v, ac.v)

            @blk.sync
            def _(sync):
                _fence(sync, F_)
                for i in range(16):
                    sync.wait_ge(dv.h, dv.v + 0)
                pass

            @blk.gpsimd
            def _(gpsimd):
                _fence(gpsimd, F_)
                for i in range(16):
                    if stage == "wq1":
                        gp1.inc(gpsimd.dma_start(out=tb16[:], in_=wq1_s[i * 128:(i + 1) * 128, 0:1024]), 16)
                    else:
                        gp1.inc(gpsimd.dma_start(out=tb16[:], in_=wq2_s[i * 64:(i + 1) * 64, 0:2048].rearrange("a b -> (a b)").rearrange("(p f) -> p f", p=128)), 16)
                    gpsimd.wait_ge(gp1.h, gp1.v)
                    gpsimd.tensor_copy(tf32[:], tb16[:])
                    v = st.inc(gpsimd.dma_start(out=y_out[i * 128:(i + 1) * 128, :], in_=tf32[:]), 16)
                    gpsimd.wait_ge(st.h, v)
                gpsimd.wait_ge(st.h, st.v)
        build_program.resid_names = {k: t.name for k, t in resid.items()}
        return nc, ctx

    for tb in range(2):
        with ExitStack() as sl:
            xln = [sl.enter_context(nc.sbuf_tensor(f"xln1_{tb}_{k}", [128, TB], BF16)) for k in range(NC)]
            H = [sl.enter_context(nc.sbuf_tensor(f"H1_{tb}_{k}", [128, TB], BF16)) for k in range(NF)]
            emit_ln(tb, xln, f"a{tb}")
            emit_mm1(tb, xln, H, f"a{tb}")
            emit_mm2a(tb, H, f"a{tb}")

    if stage == "s1":
        return debug_dump_resid()

    with ExitStack() as mam:
        wdtT = [mam.enter_context(nc.sbuf_tensor(f"wdtT{k}", [128, 1024], BF16)) for k in range(NC)]
        wxbT = [mam.enter_context(nc.sbuf_tensor(f"wxbT{k}", [128, 1024], BF16)) for k in range(NC)]

        with nc.Block() as blk:
            F_ = (pe.v, dv.v, ac.v)
            @blk.sync
            def _(sync):
                _fence(sync, F_)
                for k in range(NC):
                    sy3.inc(sync.dma_start(out=wdtT[k][:], in_=wdt_s[0:1024, k * 128:(k + 1) * 128],
                                           transpose=True), 16)
                    sy3.inc(sync.dma_start(out=wxbT[k][:], in_=wx_s[0:1024, k * 128:(k + 1) * 128],
                                           transpose=True), 16)
                sync.wait_ge(sy3.h, sy3.v)

        for s_ in range(2):
            with ExitStack() as sscope:
                ksb = [sscope.enter_context(nc.sbuf_tensor(f"k{s_}_{c}", [128, TB], BF16)) for c in range(NC)]
                invdt_bc = sscope.enter_context(nc.sbuf_tensor(f"ivd{s_}", [128, TB], F32))
                invx_bc = sscope.enter_context(nc.sbuf_tensor(f"ivx{s_}", [128, TB], F32))

                with ExitStack() as b4:
                    sq = b4.enter_context(nc.sbuf_tensor(f"m4sq{s_}", [128, 2 * TB], BF16))
                    rowb = b4.enter_context(nc.sbuf_tensor(f"m4row{s_}", [1, 4 * TB], F32))
                    xcp = b4.enter_context(nc.sbuf_tensor(f"m4xc{s_}", [128, 2 * TB], F32))
                    trs = [b4.enter_context(nc.sbuf_tensor(f"m4tr{s_}_{i}", [128, TB], F32)) for i in range(7)]
                    cmax = b4.enter_context(nc.sbuf_tensor(f"m4c{s_}", [128, 8], F32))
                    ct8 = b4.enter_context(nc.sbuf_tensor(f"m4ct{s_}", [8, 128], F32))
                    qi = b4.enter_context(nc.sbuf_tensor(f"m4qi{s_}", [128, TB], F32))
                    blk = b4.enter_context(nc.Block())
                    F_ = (pe.v, dv.v, ac.v)
                    pe0, dv0, ac0 = pe.v, dv.v, ac.v
                    gp1_0, sy3_0 = gp1.v, sy3.v
                    R = lambda i, o=0, w=TB: rowb[0:1, i * TB + o:i * TB + o + w]
                    ms = {}

                    @blk.vector
                    def _(vector):
                        _fence(vector, F_)
                        vector.wait_ge(pe.h, pe0 + 8)
                        v = dv.inc(vector.tensor_copy(R(0), pC[0:1, 0:TB]), 1)
                        v = dv.inc(vector.tensor_copy(R(1), pD[0:1, 0:TB]), 1)
                        vector.wait_ge(dv.h, v)
                        v = dv.inc(vector.tensor_scalar(R(2), R(0), INV1024, None, ALU.mult), 1)
                        v = dv.inc(vector.tensor_scalar(R(3), R(1), INV1024, None, ALU.mult), 1)
                        vector.wait_ge(dv.h, v)
                        v = dv.inc(vector.tensor_tensor(R(1), R(2), R(2), ALU.mult), 1)
                        vector.wait_ge(dv.h, v)
                        v = dv.inc(vector.tensor_tensor(R(0), R(3), R(1), ALU.subtract), 1)
                        vector.wait_ge(dv.h, v)
                        ms["vareps"] = dv.inc(vector.tensor_scalar(R(1), R(0), 1e-5, None, ALU.add), 1)
                        vector.wait_ge(pe.h, pe0 + 9)
                        for i in range(4):
                            o = 0 if i % 2 == 0 else 2
                            v = dv.inc(vector.tensor_tensor(xcp[:, 0:TB], resid[(2 * i, s_)][:],
                                                            pC[:, 0:TB], ALU.subtract), 1)
                            v = dv.inc(vector.tensor_tensor(xcp[:, TB:2 * TB], resid[(2 * i + 1, s_)][:],
                                                            pC[:, 0:TB], ALU.subtract), 1)
                            vector.wait_ge(dv.h, v)
                            vector.tensor_tensor(trs[o][:], xcp[:, 0:TB], xcp[:, TB:2 * TB], ALU.max)
                            dv.inc(vector.tensor_tensor(trs[o + 1][:], xcp[:, 0:TB], xcp[:, TB:2 * TB], ALU.min), 1)
                            if i == 1:
                                vector.tensor_tensor(trs[4][:], trs[0][:], trs[2][:], ALU.max)
                                vector.tensor_tensor(trs[5][:], trs[1][:], trs[3][:], ALU.min)
                        v = dv.inc(vector.tensor_tensor(trs[6][:], trs[0][:], trs[2][:], ALU.max), 1)
                        v = dv.inc(vector.tensor_tensor(trs[0][:], trs[1][:], trs[3][:], ALU.min), 1)
                        vector.wait_ge(dv.h, v)
                        v = dv.inc(vector.tensor_tensor(trs[1][:], trs[4][:], trs[6][:], ALU.max), 1)
                        v = dv.inc(vector.tensor_tensor(trs[2][:], trs[5][:], trs[0][:], ALU.min), 1)
                        vector.wait_ge(dv.h, v)
                        v = dv.inc(vector.tensor_scalar(trs[3][:], trs[2][:], -1.0, None, ALU.mult), 1)
                        vector.wait_ge(dv.h, v)
                        ms["m1"] = dv.inc(vector.tensor_tensor(trs[4][:], trs[1][:], trs[3][:], ALU.max), 1)
                        vector.wait_ge(pe.h, pe0 + 10)
                        inst = None
                        for jj in range(8):
                            inst = vector.tensor_reduce(cmax[:, jj:jj + 1], pD[:, jj * 128:(jj + 1) * 128],
                                                        AX, ALU.max)
                        ms["cmax"] = dv.inc(inst, 1)
                        vector.wait_ge(pe.h, pe0 + 11)
                        ms["ct8"] = dv.inc(vector.tensor_copy(ct8[:], pD[0:8, 0:128]), 1)
                        vector.wait_ge(sy3.h, sy3_0 + 16)
                        vector.wait_ge(ac.h, ac0 + 10)
                        v = dv.inc(vector.tensor_tensor(R(3), R(0), R(1), ALU.mult), 1)
                        vector.wait_ge(dv.h, v)
                        v = dv.inc(vector.tensor_scalar(R(0), R(3), 1e-5, None, ALU.max), 1)
                        vector.wait_ge(dv.h, v)
                        v = dv.inc(vector.reciprocal(R(3), R(0)), 1)
                        vector.wait_ge(dv.h, v)
                        ms["rs"] = dv.inc(vector.scalar_tensor_tensor(R(2), R(1), 127.0, R(3),
                                                                      ALU.mult, ALU.mult), 1)
                        vector.wait_ge(dv.h, ms["rs"])
                        v = dv.inc(vector.tensor_scalar(R(1), R(0), wsrow[0:1, 7:8], INV127,
                                                        ALU.mult, ALU.mult), 1)
                        ms["inv"] = dv.inc(vector.tensor_scalar(R(3), R(0), wsrow[0:1, 6:7], INV127,
                                                                ALU.mult, ALU.mult), 1)
                        vector.wait_ge(pe.h, pe0 + 12)
                        for c in range(NC):
                            v = dv.inc(vector.tensor_tensor(xcp[:, 0:TB], resid[(c, s_)][:],
                                                            pC[:, 0:TB], ALU.subtract), 1)
                            vector.wait_ge(dv.h, v)
                            v = dv.inc(vector.tensor_tensor(qi[:], xcp[:, 0:TB], pD[:, 0:TB], ALU.mult), 1)
                            vector.wait_ge(dv.h, v)
                            v = dv.inc(vector.tensor_scalar(qi[:], qi[:], MAGIC, MAGIC, ALU.add, ALU.subtract), 1)
                            vector.wait_ge(dv.h, v)
                            dv.inc(vector.tensor_scalar(ksb[c][:], qi[:], 127.0, -128.0,
                                                        ALU.min, ALU.max), 1)
                        ms["kdone"] = dv.v
                        vector.wait_ge(pe.h, pe0 + 13)
                        dv.inc(vector.tensor_copy(invdt_bc[:], pC[:, 0:TB]), 1)
                        vector.wait_ge(pe.h, pe0 + 14)
                        dv.inc(vector.tensor_copy(invx_bc[:], pD[:, 0:TB]), 1)

                    @blk.scalar
                    def _(scalar):
                        _fence(scalar, F_)
                        for c in range(NC):
                            if c >= 2:
                                scalar.wait_ge(pe.h, pe0 + c - 1)
                            ac.inc(scalar.activation(sq[:, (c % 2) * TB:(c % 2 + 1) * TB],
                                                     resid[(c, s_)][:], AF.Square), 1)
                        scalar.wait_ge(dv.h, ms["vareps"])
                        v = ac.inc(scalar.activation(R(3), R(1), AF.Ln), 1)
                        scalar.wait_ge(ac.h, v)
                        v = ac.inc(scalar.activation(R(1), R(3), AF.Exp, scale=-0.5), 1)
                        assert v == ac0 + 10

                    @blk.tensor
                    def _(tensor):
                        _fence(tensor, F_)
                        for c in range(NC):
                            tensor.wait_ge(ac.h, ac0 + c + 1)
                            rs_ = resid[(c, s_)]
                            ss = sq[:, (c % 2) * TB:(c % 2 + 1) * TB]
                            tensor.matmul(pC[0:1, 0:512], ones_cf[:], rs_[:, 0:512], start=(c == 0), stop=(c == 7))
                            tensor.matmul(pC[0:1, 512:1024], ones_cf[:], rs_[:, 512:1024], start=(c == 0), stop=(c == 7))
                            tensor.matmul(pD[0:1, 0:512], ones_cb[:], ss[:, 0:512], start=(c == 0), stop=(c == 7))
                            inst = tensor.matmul(pD[0:1, 512:1024], ones_cb[:], ss[:, 512:1024], start=(c == 0), stop=(c == 7))
                            pe.inc(inst, 1)
                        tensor.wait_ge(dv.h, ms["vareps"])
                        tensor.matmul(pC[:, 0:512], ones_r[:], R(2, 0, 512), start=True, stop=True)
                        inst = tensor.matmul(pC[:, 512:1024], ones_r[:], R(2, 512, 512), start=True, stop=True)
                        pe.inc(inst, 1)
                        tensor.wait_ge(dv.h, ms["m1"])
                        inst = None
                        for jj in range(8):
                            inst = tensor.matmul(pD[:, jj * 128:(jj + 1) * 128],
                                                 trs[4][:, jj * 128:(jj + 1) * 128], ident[:],
                                                 start=(jj in (0, 4)), stop=(jj in (3, 7)))
                        pe.inc(inst, 1)
                        tensor.wait_ge(dv.h, ms["cmax"])
                        pe.inc(tensor.matmul(pD[0:8, 0:128], cmax[:], ident[:], start=True, stop=True), 1)
                        tensor.wait_ge(dv.h, ms["rs"])
                        tensor.wait_ge(dv.h, ms["ct8"])
                        tensor.matmul(pD[:, 0:512], ones_r[:], R(2, 0, 512), start=True, stop=True)
                        inst = tensor.matmul(pD[:, 512:1024], ones_r[:], R(2, 512, 512), start=True, stop=True)
                        pe.inc(inst, 1)
                        tensor.wait_ge(dv.h, ms["kdone"])
                        tensor.matmul(pC[:, 0:512], ones_r[:], R(1, 0, 512), start=True, stop=True)
                        inst = tensor.matmul(pC[:, 512:1024], ones_r[:], R(1, 512, 512), start=True, stop=True)
                        pe.inc(inst, 1)
                        tensor.matmul(pD[:, 0:512], ones_r[:], R(3, 0, 512), start=True, stop=True)
                        inst = tensor.matmul(pD[:, 512:1024], ones_r[:], R(3, 512, 512), start=True, stop=True)
                        pe.inc(inst, 1)

                    @blk.gpsimd
                    def _(gpsimd):
                        _fence(gpsimd, F_)
                        gpsimd.wait_ge(dv.h, ms["ct8"])
                        gp1.inc(gpsimd.dma_start(out=bounce, in_=ct8[:]), 16)
                        gpsimd.wait_ge(gp1.h, gp1.v)

                    @blk.sync
                    def _(sync):
                        _fence(sync, F_)
                        sync.wait_ge(gp1.h, gp1_0 + 16)
                        sy3.inc(sync.dma_start(out=rowb[0:1, 0:TB],
                                               in_=bounce.rearrange("a b -> (a b)")[None, :]), 16)
                        sync.wait_ge(sy3.h, sy3.v)

                with ExitStack() as b5:
                    wxc = b5.enter_context(nc.sbuf_tensor(f"b5wxc{s_}", [128, 2 * 1024], BF16))
                    dt0 = b5.enter_context(nc.sbuf_tensor(f"b5dt0{s_}", [128, 512], F32))
                    b0t = b5.enter_context(nc.sbuf_tensor(f"b5b0{s_}", [128, 512], F32))
                    c0t = b5.enter_context(nc.sbuf_tensor(f"b5c0{s_}", [128, 2 * 512], F32))
                    zt = b5.enter_context(nc.sbuf_tensor(f"b5z{s_}", [128, 512], F32))
                    btt = b5.enter_context(nc.sbuf_tensor(f"b5bt{s_}", [128, 512], F32))
                    amt = b5.enter_context(nc.sbuf_tensor(f"b5am{s_}", [128, 512], F32))
                    a1t = b5.enter_context(nc.sbuf_tensor(f"b5a1{s_}", [128, 512], F32))
                    e1t = b5.enter_context(nc.sbuf_tensor(f"b5e1{s_}", [128, 512], F32))
                    elt = b5.enter_context(nc.sbuf_tensor(f"b5el{s_}", [128, 512], F32))
                    alt = b5.enter_context(nc.sbuf_tensor(f"b5al{s_}", [128, 512], F32))
                    ppt = b5.enter_context(nc.sbuf_tensor(f"b5pp{s_}", [128, 2 * 512], F32))
                    rsh = b5.enter_context(nc.sbuf_tensor(f"b5rsh{s_}", [128, 512], F32))
                    dbs = b5.enter_context(nc.sbuf_tensor(f"b5db{s_}", [128, 512], F32))
                    ut = b5.enter_context(nc.sbuf_tensor(f"b5u{s_}", [128, 2 * 512], F32))
                    uct = b5.enter_context(nc.sbuf_tensor(f"b5uc{s_}", [128, 2 * 512], F32))
                    blk = b5.enter_context(nc.Block())
                    F_ = (pe.v, dv.v, ac.v)
                    pe0, dv0, ac0 = pe.v, dv.v, ac.v
                    gp1_0, gp2_0 = gp1.v, gp2.v
                    NA = 4
                    iters = [(c, h) for c in range(NC) for h in range(2)]
                    c_sem = [sy1, sy2]
                    sy_c = [(c_sem[c % 2], c_sem[c % 2].v + (c // 2 + 1) * 128) for c in range(NC)]
                    i1, i2, i3 = [], [], []

                    @blk.vector
                    def _(vector):
                        _fence(vector, F_)
                        for q, (c, h) in enumerate(iters):
                            hs = slice(h * 512, (h + 1) * 512)
                            sl = slice((q % 2) * 512, (q % 2 + 1) * 512)
                            vector.wait_ge(pe.h, pe0 + q * 3 + 1)
                            if q >= 1:
                                vector.wait_ge(ac.h, ac0 + (q - 1) * NA + 1)
                            vector.tensor_tensor(dt0[:], pA[:, hs], invdt_bc[:, hs], ALU.mult)
                            vector.wait_ge(pe.h, pe0 + q * 3 + 2)
                            vector.tensor_tensor(b0t[:], pB[:, hs], invx_bc[:, hs], ALU.mult)
                            vector.wait_ge(pe.h, pe0 + q * 3 + 3)
                            if q >= 2:
                                vector.wait_ge(gp1.h, gp1_0 + (q - 1) * 16)
                            inst = vector.tensor_tensor(c0t[:, sl], pC[:, hs], invx_bc[:, hs], ALU.mult)
                            i1.append(dv.inc(inst, 1))
                            vector.tensor_scalar(btt[:], b0t[:], bxsb[:, c:c + 1], 5.0, ALU.add, ALU.min)
                            vector.tensor_scalar(zt[:], dt0[:], bdtsb[:, c:c + 1], None, ALU.add)
                            vector.tensor_scalar(b0t[:], btt[:], -5.0, None, ALU.max)
                            vector.wait_ge(ac.h, ac0 + q * NA + 3)
                            v = dv.inc(vector.scalar_tensor_tensor(btt[:], zt[:], 0.0, elt[:],
                                                                   ALU.max, ALU.add), 1)
                            vector.wait_ge(dv.h, v)
                            v = dv.inc(vector.tensor_scalar(zt[:], btt[:], 1.0, 0.001, ALU.min, ALU.max), 1)
                            vector.wait_ge(dv.h, v)
                            inst = vector.tensor_scalar(amt[:], zt[:], asb[:, c:c + 1], -10.0,
                                                        ALU.mult, ALU.max)
                            i2.append(dv.inc(inst, 1))
                            vector.wait_ge(ac.h, ac0 + q * NA + 4)
                            if q >= 2:
                                vector.wait_ge(gp2.h, gp2_0 + (q - 1) * 32)
                            v = dv.inc(vector.scalar_tensor_tensor(dbs[:], zt[:], S64, b0t[:],
                                                                   ALU.mult, ALU.mult), 1)
                            if h == 0:
                                initP = chainP[:, c:c + 1]
                                initS = chainS[:, c:c + 1]
                            else:
                                initP = ppt[:, ((q - 1) % 2) * 512 + 511:((q - 1) % 2) * 512 + 512]
                                initS = uct[:, ((q - 1) % 2) * 512 + 511:((q - 1) % 2) * 512 + 512]
                            v = dv.inc(vector.tensor_tensor_scan(ppt[:, sl], alt[:], zeros_h[:],
                                                                 initP, ALU.mult, ALU.add), 1)
                            vector.wait_ge(dv.h, v)
                            vector.reciprocal(rsh[:, 0:1], initP)
                            v = dv.inc(vector.reciprocal(rsh[:, 1:512],
                                                         ppt[:, (q % 2) * 512:(q % 2) * 512 + 511]), 1)
                            vector.wait_ge(dv.h, v)
                            v = dv.inc(vector.tensor_tensor(ut[:, sl], dbs[:], rsh[:], ALU.mult), 1)
                            vector.wait_ge(dv.h, v)
                            inst = vector.tensor_tensor_scan(uct[:, sl], ut[:, sl], zeros_h[:],
                                                             initS, ALU.add, ALU.add)
                            v = dv.inc(inst, 1)
                            if h == 1:
                                vector.wait_ge(dv.h, v)
                                if s_ == 0:
                                    vector.tensor_copy(chainP[:, c:c + 1],
                                                       ppt[:, (q % 2) * 512 + 511:(q % 2) * 512 + 512])
                                    v = dv.inc(vector.tensor_copy(chainS[:, c:c + 1],
                                                                  uct[:, (q % 2) * 512 + 511:(q % 2) * 512 + 512]), 1)
                                else:
                                    vector.tensor_copy(ends[:, c:c + 1],
                                                       ppt[:, (q % 2) * 512 + 511:(q % 2) * 512 + 512])
                                    v = dv.inc(vector.tensor_copy(ends[:, 8 + c:9 + c],
                                                                  uct[:, (q % 2) * 512 + 511:(q % 2) * 512 + 512]), 1)
                            i3.append(v)

                    @blk.scalar
                    def _(scalar):
                        _fence(scalar, F_)
                        for q, (c, h) in enumerate(iters):
                            scalar.wait_ge(dv.h, i1[q])
                            v = ac.inc(scalar.activation(a1t[:], dt0[:], AF.Abs, bias=bdtsb[:, c:c + 1]), 1)
                            scalar.wait_ge(ac.h, v)
                            v = ac.inc(scalar.activation(e1t[:], a1t[:], AF.Exp, scale=-1.0), 1)
                            scalar.wait_ge(ac.h, v)
                            ac.inc(scalar.activation(elt[:], e1t[:], AF.Ln, bias=1.0), 1)
                            scalar.wait_ge(dv.h, i2[q])
                            if q >= 1:
                                scalar.wait_ge(dv.h, i3[q - 1])
                            ac.inc(scalar.activation(alt[:], amt[:], AF.Exp), 1)

                    @blk.tensor
                    def _(tensor):
                        _fence(tensor, F_)
                        for q, (c, h) in enumerate(iters):
                            if q >= 1:
                                tensor.wait_ge(dv.h, i1[q - 1])
                            if h == 0:
                                tensor.wait_ge(sy_c[c][0].h, sy_c[c][1])
                            hs = slice(h * 512, (h + 1) * 512)
                            inst = None
                            for k in range(NC):
                                inst = tensor.matmul(pA[:, hs], wdtT[k][:, c * 128:(c + 1) * 128],
                                                     ksb[k][:, hs], start=(k == 0), stop=(k == 7))
                            pe.inc(inst, 1)
                            inst = None
                            for k in range(NC):
                                inst = tensor.matmul(pB[:, hs], wxbT[k][:, c * 128:(c + 1) * 128],
                                                     ksb[k][:, hs], start=(k == 0), stop=(k == 7))
                            pe.inc(inst, 1)
                            inst = None
                            for k in range(NC):
                                inst = tensor.matmul(pC[:, hs],
                                                     wxc[:, (c % 2) * 1024 + k * 128:(c % 2) * 1024 + (k + 1) * 128],
                                                     ksb[k][:, hs], start=(k == 0), stop=(k == 7))
                            pe.inc(inst, 1)

                    @blk.gpsimd
                    def _(gpsimd):
                        _fence(gpsimd, F_)
                        for q, (c, h) in enumerate(iters):
                            row0 = (s_ * 8 + c) * 128
                            hs = slice(h * 512, (h + 1) * 512)
                            sl = slice((q % 2) * 512, (q % 2 + 1) * 512)
                            gpsimd.wait_ge(dv.h, i1[q])
                            gp1.inc(gpsimd.dma_start(out=c_sp[row0:row0 + 128, hs], in_=c0t[:, sl]), 16)
                            gpsimd.wait_ge(dv.h, i3[q])
                            gp2.inc(gpsimd.dma_start(out=u_sp[row0:row0 + 128, hs], in_=ut[:, sl]), 16)
                            gp2.inc(gpsimd.dma_start(out=p_sp[row0:row0 + 128, hs], in_=ppt[:, sl]), 16)
                        gpsimd.wait_ge(gp1.h, gp1.v)
                        gpsimd.wait_ge(gp2.h, gp2.v)

                    @blk.sync
                    def _(sync):
                        _fence(sync, F_)
                        for c in range(NC):
                            if c >= 2:
                                sync.wait_ge(pe.h, pe0 + 6 * (c - 1))
                            v = None
                            for k in range(NC):
                                v = c_sem[c % 2].inc(sync.dma_start(
                                    out=wxc[:, (c % 2) * 1024 + k * 128:(c % 2) * 1024 + (k + 1) * 128],
                                    in_=wx_s[1024 + c * 128:1024 + (c + 1) * 128, k * 128:(k + 1) * 128],
                                    transpose=True), 16)
                            assert (c_sem[c % 2], v) == sy_c[c]
                        for sem in (sy1, sy2):
                            sync.wait_ge(sem.h, sem.v)

        with ExitStack() as b6:
            gath = b6.enter_context(nc.sbuf_tensor("gath", [128, 128], F32))
            part = b6.enter_context(nc.sbuf_tensor("part", [128, 16], F32))
            ptmp = b6.enter_context(nc.sbuf_tensor("ptmp", [128, 96], F32))
            blk = b6.enter_context(nc.Block())
            F_ = (pe.v, dv.v, ac.v)
            gp1_0, gp2_0, dv0 = gp1.v, gp2.v, dv.v

            @blk.gpsimd
            def _(gpsimd):
                _fence(gpsimd, F_)
                gp1.inc(gpsimd.dma_start(out=cc_in.ap(), in_=ends[:]), 16)
                gpsimd.wait_ge(gp1.h, gp1_0 + 16)
                v = ccs.inc(gpsimd.collective_compute(
                    "AllGather", ALU.bypass, replica_groups=[list(range(8))],
                    ins=[cc_in.ap().opt()], outs=[cc_out.ap().opt()]), 1)
                gpsimd.wait_ge(ccs.h, v)
                for r in range(8):
                    gp2.inc(gpsimd.dma_start(out=gath[:, r * 16:(r + 1) * 16],
                                             in_=cc_out.ap()[r * 128:(r + 1) * 128, :]), 16)
                gpsimd.wait_ge(gp2.h, gp2.v)

            @blk.vector
            def _(vector):
                _fence(vector, F_)
                vector.wait_ge(gp2.h, gp2_0 + 8 * 16)
                for i in range(4):
                    dv.inc(vector.select(ptmp[:, i * 16:(i + 1) * 16], sbits[:, 0:16],
                                         gath[:, (2 * i + 1) * 16:(2 * i + 2) * 16],
                                         gath[:, (2 * i) * 16:(2 * i + 1) * 16]), 1)
                vector.wait_ge(dv.h, dv.v)
                for i in range(2):
                    dv.inc(vector.select(ptmp[:, 64 + i * 16:64 + (i + 1) * 16], sbits[:, 16:32],
                                         ptmp[:, (2 * i + 1) * 16:(2 * i + 2) * 16],
                                         ptmp[:, (2 * i) * 16:(2 * i + 1) * 16]), 1)
                vector.wait_ge(dv.h, dv.v)
                v = dv.inc(vector.select(part[:], sbits[:, 32:48], ptmp[:, 80:96], ptmp[:, 64:80]), 1)
                vector.wait_ge(dv.h, v)
                for c in range(NC):
                    v = dv.inc(vector.select(ptmp[:, 0:1], musb[:], part[:, c:c + 1], bigcol[:]), 1)
                    vector.wait_ge(dv.h, v)
                    v = dv.inc(vector.reciprocal(ptmp[:, 1:2], ptmp[:, 0:1]), 1)
                    vector.wait_ge(dv.h, v)
                    dv.inc(vector.tensor_scalar(r1s_t[:, c:c + 1], ptmp[:, 1:2], S64, None, ALU.mult), 1)
                    dv.inc(vector.tensor_scalar(q2_t[:, c:c + 1], ptmp[:, 0:1], S64I, None, ALU.mult), 1)
                    dv.inc(vector.select(chainS2[:, c:c + 1], musb[:], part[:, 8 + c:9 + c], zcol[:]), 1)
                dv.inc(vector.tensor_copy(ptmp[:, 0:1], ptmp[:, 0:1]), 1)

        with ExitStack() as b7:
            c0r = b7.enter_context(nc.sbuf_tensor("b7c0", [128, 2 * 512], F32))
            ur = b7.enter_context(nc.sbuf_tensor("b7u", [128, 2 * 512], F32))
            ppr = b7.enter_context(nc.sbuf_tensor("b7pp", [128, 2 * 512], F32))
            ctt = b7.enter_context(nc.sbuf_tensor("b7ct", [128, 512], F32))
            ct2 = b7.enter_context(nc.sbuf_tensor("b7ct2", [128, 512], F32))
            e2t = b7.enter_context(nc.sbuf_tensor("b7e2", [128, 512], F32))
            tht = b7.enter_context(nc.sbuf_tensor("b7th", [128, 512], F32))
            gt = b7.enter_context(nc.sbuf_tensor("b7g", [128, 512], F32))
            cst = b7.enter_context(nc.sbuf_tensor("b7cs", [128, 2 * 512], F32))
            q1t = b7.enter_context(nc.sbuf_tensor("b7q1", [128, 512], F32))
            tqt = b7.enter_context(nc.sbuf_tensor("b7tq", [128, 512], F32))
            htt = b7.enter_context(nc.sbuf_tensor("b7h", [128, 512], F32))
            yt = b7.enter_context(nc.sbuf_tensor("b7y", [128, 512], F32))
            ryt = b7.enter_context(nc.sbuf_tensor("b7ry", [128, 512], F32))
            mkt = b7.enter_context(nc.sbuf_tensor("b7mk", [128, 512], I32))
            ayt = b7.enter_context(nc.sbuf_tensor("b7ay", [128, 512], F32))
            l1t = b7.enter_context(nc.sbuf_tensor("b7l1", [128, 512], F32))
            emt = b7.enter_context(nc.sbuf_tensor("b7em", [128, 512], F32))
            blk = b7.enter_context(nc.Block())
            F_ = (pe.v, dv.v, ac.v)
            dv0, ac0 = dv.v, ac.v
            NA7 = 4
            iters = [(si, c, h) for si in range(2) for c in range(NC) for h in range(2)]
            r_sem = [sy1, sy2]
            sy_r = [(r_sem[q % 2], r_sem[q % 2].v + (q // 2 + 1) * 48) for q in range(32)]
            i1, i2, i3 = [], [], []

            @blk.vector
            def _(vector):
                _fence(vector, F_)
                for q, (si, c, h) in enumerate(iters):
                    sl = slice((q % 2) * 512, (q % 2 + 1) * 512)
                    vector.wait_ge(sy_r[q][0].h, sy_r[q][1])
                    if q >= 1:
                        vector.wait_ge(ac.h, ac0 + (q - 1) * NA7 + 1)
                    vector.tensor_scalar(ctt[:], c0r[:, sl], bxsb[:, 8 + c:9 + c], 5.0,
                                         ALU.add, ALU.min)
                    vector.tensor_scalar(gt[:], ur[:, sl], r1s_t[:, c:c + 1], None, ALU.mult)
                    inst = vector.tensor_scalar(ct2[:], ctt[:], -5.0, None, ALU.max)
                    i1.append(dv.inc(inst, 1))
                    vector.wait_ge(ac.h, ac0 + q * NA7 + 1)
                    vector.tensor_scalar(q1t[:], ppr[:, sl], S64I, None, ALU.mult)
                    if h == 0:
                        initS = chainS2[:, c:c + 1]
                    else:
                        initS = cst[:, ((q - 1) % 2) * 512 + 511:((q - 1) % 2) * 512 + 512]
                    v = dv.inc(vector.tensor_tensor_scan(cst[:, sl], gt[:], zeros_h[:],
                                                         initS, ALU.add, ALU.add), 1)
                    vector.wait_ge(dv.h, v)
                    vector.wait_ge(ac.h, ac0 + q * NA7 + 3)
                    vector.tensor_scalar(tht[:], emt[:], -2.0, 1.0, ALU.mult, ALU.add)
                    v = dv.inc(vector.tensor_tensor(tqt[:], q1t[:], cst[:, sl], ALU.mult), 1)
                    vector.wait_ge(dv.h, v)
                    v = dv.inc(vector.tensor_scalar(htt[:], tqt[:], q2_t[:, c:c + 1], None, ALU.mult), 1)
                    vector.wait_ge(dv.h, v)
                    inst = vector.tensor_tensor(yt[:], htt[:], tht[:], ALU.mult)
                    i2.append(dv.inc(inst, 1))
                    vector.wait_ge(dv.h, i2[q])
                    rs_ = resid[(c, si)][:, h * 512:(h + 1) * 512]
                    sp_ = spare[0][:, h * 512:(h + 1) * 512]
                    vector.tensor_tensor(ryt[:], rs_, yt[:], ALU.add)
                    if si == 0 and h == 1:
                        vector.tensor_copy(chainS2[:, c:c + 1],
                                           cst[:, (q % 2) * 512 + 511:(q % 2) * 512 + 512])
                    vector.wait_ge(ac.h, ac0 + q * NA7 + 4)
                    v = dv.inc(vector.tensor_scalar(mkt[:], ayt[:], 3.4028234663852886e38,
                                                    None, ALU.is_le), 1)
                    vector.wait_ge(dv.h, v)
                    inst = vector.select(sp_, mkt[:], ryt[:], rs_)
                    i3.append(dv.inc(inst, 1))
                    if h == 1:
                        old = resid[(c, si)]
                        resid[(c, si)] = spare[0]
                        spare[0] = old

            @blk.sync
            def _(sync):
                _fence(sync, F_)
                for q, (si, c, h) in enumerate(iters):
                    row0 = (si * 8 + c) * 128
                    hs = slice(h * 512, (h + 1) * 512)
                    if q >= 2:
                        sync.wait_ge(dv.h, i2[q - 2])
                    sl = slice((q % 2) * 512, (q % 2 + 1) * 512)
                    sem = r_sem[q % 2]
                    sem.inc(sync.dma_start(out=c0r[:, sl], in_=c_sp[row0:row0 + 128, hs]), 16)
                    sem.inc(sync.dma_start(out=ur[:, sl], in_=u_sp[row0:row0 + 128, hs]), 16)
                    v = sem.inc(sync.dma_start(out=ppr[:, sl], in_=p_sp[row0:row0 + 128, hs]), 16)
                    assert (sem, v) == sy_r[q]
                for sem in (sy1, sy2):
                    sync.wait_ge(sem.h, sem.v)

            @blk.scalar
            def _(scalar):
                _fence(scalar, F_)
                for q, (si, c, h) in enumerate(iters):
                    scalar.wait_ge(dv.h, i1[q])
                    v = ac.inc(scalar.activation(e2t[:], ct2[:], AF.Exp, scale=2.0), 1)
                    scalar.wait_ge(ac.h, v)
                    v = ac.inc(scalar.activation(l1t[:], e2t[:], AF.Ln, bias=1.0), 1)
                    scalar.wait_ge(ac.h, v)
                    ac.inc(scalar.activation(emt[:], l1t[:], AF.Exp, scale=-1.0), 1)
                    scalar.wait_ge(dv.h, i2[q])
                    ac.inc(scalar.activation(ayt[:], yt[:], AF.Abs), 1)

    if stage == "s2":
        return debug_dump_resid()

    for tb in range(2):
        with ExitStack() as sl:
            xln = [sl.enter_context(nc.sbuf_tensor(f"xln3_{tb}_{k}", [128, TB], BF16)) for k in range(NC)]
            H = [sl.enter_context(nc.sbuf_tensor(f"H3_{tb}_{k}", [128, TB], BF16)) for k in range(NF)]
            emit_ln(tb, xln, f"c{tb}")
            emit_mm1(tb, xln, H, f"c{tb}")
            emit_mm2b(tb, H, f"c{tb}")

    build_program.resid_names = {k: t.name for k, t in resid.items()}
    return nc, ctx


def kernel(**inputs):
    x = np.asarray(inputs["L0_attn_x"], dtype=np.float32)
    W_x = np.asarray(inputs["L0_attn_W_x"], dtype=np.float32)
    b_x = np.asarray(inputs["L0_attn_b_x"], dtype=np.float32)
    W_dt = np.asarray(inputs["L0_attn_W_dt"], dtype=np.float32)
    b_dt = np.asarray(inputs["L0_attn_b_dt"], dtype=np.float32)
    A_log = np.asarray(inputs["L0_attn_A_log"], dtype=np.float32)
    W1 = np.asarray(inputs["L0_ffn_W1"], dtype=np.float32)
    b1 = np.asarray(inputs["L0_ffn_b1"], dtype=np.float32)
    W2 = np.asarray(inputs["L0_ffn_W2"], dtype=np.float32)
    b2 = np.asarray(inputs["L0_ffn_b2"], dtype=np.float32)

    nc, ctx = build_program()

    ident = np.eye(128, dtype=np.float32)
    in_maps = []
    for core in range(8):
        b, j = core // 2, core % 2
        partner = (core - 1) % 8
        mu = np.full((128, 1), 1 if j == 1 else 0, np.int32)
        sb = np.zeros((128, 48), np.int32)
        for bit in range(3):
            sb[:, bit * 16:(bit + 1) * 16] = (partner >> bit) & 1
        gsl = np.concatenate([
            W1[core * 512:(core + 1) * 512, :],
            W2[core * 128:(core + 1) * 128, :].reshape(512, 1024),
            W_x[core * 256:(core + 1) * 256, :],
            W_dt[core * 128:(core + 1) * 128, :],
        ], axis=0)
        in_maps.append({
            "x_in": np.ascontiguousarray(x[b, j * TPC:(j + 1) * TPC, :]),
            "w1_in": W1, "b1_in": b1, "w2_in": W2, "b2_in": b2,
            "wx_in": W_x, "bx_in": b_x, "wdt_in": W_dt, "bdt_in": b_dt,
            "alog_in": A_log, "ident_in": ident, "mu_in": mu, "sbit_in": sb,
            "gsl_in": np.ascontiguousarray(gsl),
        })

    res = run_bass_kernel_spmd(nc, in_maps, core_ids=list(range(8)))

    out = np.empty((4, 4096, 1024), dtype=np.float32)
    for core in range(8):
        b, j = core // 2, core % 2
        out[b, j * TPC:(j + 1) * TPC, :] = res.results[core]["y_out"]
    return out
